# version 23
# baseline (speedup 1.0000x reference)
import time
import numpy as np
import concourse.bass as bass
import concourse.mybir as mybir
from concourse.bass_utils import run_bass_kernel_spmd

F32 = mybir.dt.float32
BF16 = mybir.dt.bfloat16
I16 = mybir.dt.int16
I32 = mybir.dt.int32
AF = mybir.ActivationFunctionType
ALU = mybir.AluOpType
AX = mybir.AxisListType

NCORE = 8
D = 1024
TOWN = 2048
TKV = 2176
NT = 16
NTK = 17
NH = 16
II = 3072
NI = 24
NDC = 8
INV_QA = 1.0 / 32767.0
INV_QW = 1.0 / 7.0
MAGIC = 12582912.0

S0, R0, SUP, RUP, SG, RG, SG2, RG2, SO, RO, SGQ = range(11)
WTILES = [(NI, 4, 1), (NI, 4, 1), (NDC, 12, 3)]
NWTL = NI + NI + NDC
NWSUB = NI + NI + NDC * 3


def _wmeta():
    g, sb = 0, 0
    out = []
    for widx, (ntl, nb, nsub) in enumerate(WTILES):
        for ti in range(ntl):
            out.append((g, widx, ti, nb, nb * 32, nsub, sb))
            g += 1
            sb += nsub
    return out


WMETA = _wmeta()


def build_program(debug=False):
    nc = bass.Bass("TRN2", target_bir_lowering=False,
                   detect_race_conditions=False,
                   name=f"mk_{time.time_ns() % 1000000000}")

    x_sh = nc.dram_tensor("x_sh", [TKV, D], F32, kind="ExternalInput")
    wT = {k: nc.dram_tensor(f"w{k}T", [D, D], F32, kind="ExternalInput")
          for k in "qkvo"}
    up_sl = nc.dram_tensor("up_sl", [II, 128], F32, kind="ExternalInput")
    gate_sl = nc.dram_tensor("gate_sl", [II, 128], F32, kind="ExternalInput")
    down_sl = nc.dram_tensor("down_sl", [D, 384], F32, kind="ExternalInput")
    etab = nc.dram_tensor("etab", [NH * 3 * 128, 128], F32, kind="ExternalInput")
    ident_d = nc.dram_tensor("ident_d", [128, 128], F32, kind="ExternalInput")

    out_sh = nc.dram_tensor("out_sh", [TOWN, D], F32, kind="ExternalOutput")
    dbg = {}
    if debug:
        dbg["sc"] = nc.dram_tensor("dbg_sc", [1, 16], F32, kind="ExternalOutput")
        dbg["att"] = nc.dram_tensor("dbg_att", [TOWN, D], F32, kind="ExternalOutput")
        dbg["xn"] = nc.dram_tensor("dbg_xn", [TKV, D], BF16, kind="ExternalOutput")
        dbg["q"] = nc.dram_tensor("dbg_q", [TOWN, D], BF16, kind="ExternalOutput")
        dbg["k"] = nc.dram_tensor("dbg_k", [TKV, D], BF16, kind="ExternalOutput")
        dbg["ug"] = nc.dram_tensor("dbg_ug", [D, II], BF16, kind="ExternalOutput")
        dbg["dg"] = nc.dram_tensor("dbg_dg", [II, D], BF16, kind="ExternalOutput")
        dbg["raw"] = nc.dram_tensor("dbg_raw", [256, TOWN], F32, kind="ExternalOutput")

    d_xn = nc.dram_tensor("d_xn", [TKV, D], BF16)
    d_q = nc.dram_tensor("d_q", [TOWN, D], BF16)
    d_k = nc.dram_tensor("d_k", [TKV, D], BF16)
    d_o = nc.dram_tensor("d_o", [TOWN, D], BF16)
    d_att = nc.dram_tensor("d_att", [TOWN, D], F32)
    dkind = dict(kind="ExternalOutput") if debug else {}
    d_upraw = nc.dram_tensor("d_upraw", [II, TOWN], F32, **dkind)
    d_gateraw = nc.dram_tensor("d_gateraw", [II, TOWN], F32, **dkind)
    d_gint = nc.dram_tensor("d_gint", [II, TOWN], I16)
    d_g2int = nc.dram_tensor("d_g2int", [II, TOWN], I16, **dkind)
    d_pr = [nc.dram_tensor(f"d_pr{i}", [128, 2], F32) for i in range(4)]
    cc_in = [nc.dram_tensor(f"cc_in{i}", [1, 8], F32) for i in range(4)]
    cc_out = [nc.dram_tensor(f"cc_out{i}", [NCORE, 8], F32, addr_space="Shared")
              for i in range(4)]
    ug_in = nc.dram_tensor("ug_in", [128, II], BF16)
    ug_out = nc.dram_tensor("ug_out", [D, II], BF16, addr_space="Shared")
    gg_in = nc.dram_tensor("gg_in", [128, II], BF16)
    gg_out = nc.dram_tensor("gg_out", [D, II], BF16, addr_space="Shared")
    dg_in = nc.dram_tensor("dg_in", [384, D], BF16)
    dg_out = nc.dram_tensor("dg_out", [II, D], BF16, addr_space="Shared")
    W_DRAM = [up_sl, gate_sl, down_sl]
    W_AGIN = [ug_in, gg_in, dg_in]

    from contextlib import ExitStack
    with ExitStack() as ctx:
        sbt = lambda n, s, d: ctx.enter_context(nc.sbuf_tensor(n, s, d))
        pst = lambda n, s: ctx.enter_context(nc.psum_tensor(n, s, F32))
        sem = lambda n: ctx.enter_context(nc.semaphore(n))

        E = sbt("E", [128, NH * 3 * 128], F32)
        arena = sbt("arena", [128, 51200], I16)
        ab = arena[:, :].bitcast(BF16)
        V = sbt("V", [128, NTK * NH * 65], BF16)
        wbuf = sbt("wbuf", [128, 8192], BF16)
        xt = [sbt("xt0", [128, D], F32), sbt("xt1", [128, D], F32)]
        f32a = sbt("f32a", [128, 2048], F32)
        f32b = sbt("f32b", [128, 1024], F32)
        i16s = sbt("i16s", [128, 2048], I16)
        ot = [sbt("ot0", [128, 512], BF16), sbt("ot1", [128, 512], BF16),
              None, None]
        O_sb = sbt("O_sb", [128, 1024], BF16)
        PT = [sbt("PT0", [128, 256], BF16), sbt("PT1", [128, 256], BF16),
              sbt("PT2", [128, 256], BF16), sbt("PT3", [128, 256], BF16)]
        stats = sbt("stats", [128, 16], F32)
        m0acc = sbt("m0acc", [128, NTK], F32)
        mupacc = sbt("mupacc", [128, NI], F32)
        mgacc = sbt("mgacc", [128, NI], F32)
        mg2acc = sbt("mg2acc", [128, 2 * NI], F32)
        moacc = sbt("moacc", [128, 2 * NI], F32)
        red2 = sbt("red2", [128, 2], F32)
        rowb = sbt("rowb", [1, 256], F32)
        garow = sbt("garow", [1, 64], F32)
        locm = sbt("locm", [1, 8], F32)
        sc_row = sbt("sc_row", [1, 16], F32)
        SC = sbt("SC", [128, 16], F32)
        ones_f = sbt("ones_f", [1, 128], F32)
        epsb = sbt("epsb", [128, 1], F32)
        ss = sbt("ss", [128, 32], F32)
        recs = sbt("recs", [128, 32], F32)
        idn = sbt("idn", [128, 128], BF16)
        magp = sbt("magp", [128, 1], F32)
        magn = sbt("magn", [128, 1], F32)
        nc.const_aps.register(F32, MAGIC, magp[:, 0:1])
        nc.const_aps.register(F32, -MAGIC, magn[:, 0:1])
        af = arena[:, :].bitcast(F32)
        wfl2 = [af[:, 9216:9600], af[:, 9600:9984]]
        wqs2 = [af[:, 9984:10032], af[:, 10032:10080]]
        weff2 = [O_sb[:, 0:384], O_sb[:, 384:768]]
        tst = [PT[0][:, 0:128], PT[1][:, 0:128]]
        Eal = E[:, 0:2048]
        Fal2 = E[:, 2048:4096]
        Vf = V[:, 0:12288].bitcast(F32)
        Vi = V[:, :].bitcast(I16)
        upst = [f32a[:, :], Vf[:, 0:2048]]
        gast = [Eal, Vf[:, 2048:4096]]
        UP2 = [f32a[:, 0:1024], f32a[:, 1024:2048],
               Vf[:, 0:1024], Vf[:, 1024:2048]]
        GA2 = [Eal[:, 0:1024], Eal[:, 1024:2048],
               Vf[:, 2048:3072], Vf[:, 3072:4096]]
        T12 = [Fal2[:, 0:1024], Fal2[:, 1024:2048],
               Vf[:, 4096:5120], Vf[:, 5120:6144]]
        wbf = wbuf[:, :].bitcast(F32)
        T12a = [wbf[:, 0:1024], wbf[:, 1024:2048],
                wbf[:, 2048:3072], wbf[:, 3072:4096]]
        GR3 = [Eal[:, 0:1024], Eal[:, 1024:2048],
               Vf[:, 0:1024], Vf[:, 1024:2048]]
        T13 = [Fal2[:, 0:1024], Fal2[:, 1024:2048],
               Vf[:, 2048:3072], Vf[:, 3072:4096]]
        UQ3 = [f32a[:, 0:1024], f32a[:, 1024:2048],
               Vf[:, 4096:5120], Vf[:, 5120:6144]]
        I163 = [i16s[:, 0:1024], i16s[:, 1024:2048],
                Vi[:, 12288:13312], Vi[:, 13312:14336]]
        UP2p = [f32a[:, 0:2048], Vf[:, 0:2048]]
        GA2p = [Eal, Vf[:, 2048:4096]]
        GR3p = [Eal, Vf[:, 0:2048]]
        I163p = [i16s[:, 0:2048], Vi[:, 12288:14336]]
        QA4 = [Eal, Fal2, Vf[:, 0:2048], Vf[:, 2048:4096]]
        ot[2] = O_sb[:, 0:512]
        ot[3] = O_sb[:, 512:1024]

        XNc = 33792
        KTc = 16384

        pp = [pst(f"pp{i}", [128, 512]) for i in range(4)]
        pbO = pst("pbO", [128, 1536])
        psp = pst("psp", [128, 512])

        def e4bank(s, hf):
            k = s * 2 + hf
            if k < 4:
                return pp[k][:, :]
            if k < 7:
                return pbO[:, (k - 4) * 512:(k - 3) * 512]
            return psp[:, :]

        s_x = [sem("s_x0"), sem("s_x1")]
        g_idl = sem("g_idl")
        g_wsl = [sem("g_wsl0"), sem("g_wsl1")]
        s_e = sem("s_e")
        s_tr = sem("s_tr")
        s_wq = sem("s_wq")
        g_gis = sem("g_gis")
        s_wg = [sem("s_wg0"), sem("s_wg1")]
        s_wd = [sem("s_wd0"), sem("s_wd1")]
        s_gl = [sem("s_gl0"), sem("s_gl1")]
        s_rru = sem("s_rru")
        s_rrg = sem("s_rrg")
        s_gil = sem("s_gil")
        s_att = sem("s_att")
        g_w = sem("g_w")
        g_wag = [sem("g_wag0"), sem("g_wag1")]
        g_xn = sem("g_xn")
        g_qk = sem("g_qk")
        g_o = sem("g_o")
        g_attw = sem("g_attw")
        g_raw = sem("g_raw")
        g_gi = sem("g_gi")
        g_g2 = sem("g_g2")
        g_st = sem("g_st")
        g_pr = sem("g_pr")
        c_w = [sem(f"c_w{i}") for i in range(3)]
        c_m = [sem(f"c_m{i}") for i in range(4)]
        w_v = sem("w_v")
        w_t = sem("w_t")
        w_e = sem("w_e")
        a_v = sem("a_v")
        a_s = sem("a_s")
        p_mm = sem("p_mm")
        p_ev = sem("p_ev")
        l2_v = sem("l2_v")
        l2_s = sem("l2_s")
        b_mm = sem("b_mm")
        b_exp = sem("b_exp")
        b_pt = sem("b_pt")
        b_av = sem("b_av")
        b_on = sem("b_on")
        xq_v = sem("xq_v")
        xq_s = sem("xq_s")
        e_mm = sem("e_mm")
        e_ev = sem("e_ev")
        e_v = sem("e_v")
        e2s = sem("e2s")
        e2v = sem("e2v")
        e3s = sem("e3s")
        e3v = sem("e3v")
        e4s = sem("e4s")
        e4v = sem("e4v")
        d_mm = sem("d_mm")
        f_ev = sem("f_ev")
        mrv = sem("mrv")
        scb = sem("scb")
        vs = sem("vs")
        ssf = sem("ssf")
        vsn = [0]
        ssn = [0]

        def vst(ins):
            vsn[0] += 1
            ins.then_inc(vs, 1)

        def sst(ins):
            ssn[0] += 1
            ins.then_inc(ssf, 1)

        GW_P = {p: 128 * (i + 1) for i, p in enumerate("kvo")}

        def zpos(npc, c, k):
            p = c % 2
            return npc * (c - p) + 2 * k + p + 1
        NI2 = 2 * NI
        zq4 = lambda npc, u, k: npc * (u - u % 4) + 4 * k + (u % 4) + 1
        E2V = lambda u, k: zq4(5, u, k)
        E2S = lambda u, k: zq4(4, u, k)
        E3V = lambda u, k: zq4(2, u, k)
        E3S = lambda u, k: zq4(3, u, k)

        def rawsl(d, u):
            return d.ap()[(u // 2) * 128:(u // 2 + 1) * 128,
                          (u % 2) * 1024:(u % 2) * 1024 + 1024]
        E4V = lambda c, k: zq4(3, c, k)
        E4S = lambda c, k: zq4(2, c, k)
        PROJ_BASE = {"q": 0, "k": NT, "v": NT + NTK, "o": NT + NTK + NTK}
        AX0 = (NTK + 1) // 2
        AX1 = NTK // 2

        block = ctx.enter_context(nc.Block())

        @block.sync
        def _(sy):
            for i in range(NTK):
                if i >= 2:
                    sy.wait_ge(a_v, (i - 2) * 5 + 4)
                    sy.wait_ge(a_s, (i - 2) * 2 + 1)
                sy.dma_start(out=xt[i % 2][:, :],
                             in_=x_sh.ap()[i * 128:(i + 1) * 128, :]
                             ).then_inc(s_x[i % 2], 16)
                if i == 11:
                    sy.wait_ge(g_xn, 16 * 8)
                    for j in range(NDC):
                        sy.dma_start(out=ab[:, XNc + j * TKV:
                                            XNc + j * TKV + 1024],
                                     in_=d_xn.ap()[0:1024, j * 128:(j + 1) * 128],
                                     transpose=True).then_inc(s_tr, 16)
            sy.wait_ge(g_xn, 16 * NTK)
            for j in range(NDC):
                sy.dma_start(out=ab[:, XNc + j * TKV + 1024:
                                    XNc + (j + 1) * TKV],
                             in_=d_xn.ap()[1024:TKV, j * 128:(j + 1) * 128],
                             transpose=True).then_inc(s_tr, 16)
            for h in range(NH):
                for s in range(3):
                    c0 = (h * 3 + s) * 128
                    sy.dma_start(out=E[:, c0:c0 + 128],
                                 in_=etab.ap()[c0:c0 + 128, :]).then_inc(s_e, 16)
            sy.wait_ge(g_qk, 16 * (NT + NTK))
            for j in range(NDC):
                sy.dma_start(out=ab[:, j * TOWN:(j + 1) * TOWN],
                             in_=d_q.ap()[:, j * 128:(j + 1) * 128],
                             transpose=True).then_inc(s_tr, 16)
            for j in range(NDC):
                sy.dma_start(out=ab[:, KTc + j * TKV: KTc + (j + 1) * TKV],
                             in_=d_k.ap()[:, j * 128:(j + 1) * 128],
                             transpose=True).then_inc(s_tr, 16)
            sy.wait_ge(g_o, 16 * NT)
            for j in range(NDC):
                sy.dma_start(out=ab[:, j * TOWN:(j + 1) * TOWN],
                             in_=d_o.ap()[:, j * 128:(j + 1) * 128],
                             transpose=True).then_inc(s_tr, 16)
            sy.wait_ge(c_w[0], 1)
            sy.wait_ge(c_w[1], 1)
            sy.wait_ge(p_mm, PROJ_BASE["o"] + NT)
            for c in range(NI):
                if c >= 2:
                    sy.wait_ge(e_mm, c - 1)
                base = (c % 2) * 4096
                sy.dma_start(out=wbuf[:, base:base + 1024].rearrange(
                                 "p (j i) -> p j i", i=128),
                             in_=ug_out.ap()[:, c * 128:(c + 1) * 128].rearrange(
                                 "(j p) i -> p j i", p=128)
                             ).then_inc(s_wg[c % 2], 16)
                sy.dma_start(out=wbuf[:, base + 1024:base + 2048].rearrange(
                                 "p (j i) -> p j i", i=128),
                             in_=gg_out.ap()[:, c * 128:(c + 1) * 128].rearrange(
                                 "(j p) i -> p j i", p=128)
                             ).then_inc(s_wg[c % 2], 16)
            sy.wait_ge(g_raw, 16 * 2 * NI)
            for c in range(NI):
                if c >= 2 and (c - 2) % 2 == 1:
                    j = c - 2
                    sy.wait_ge(g_gis, 16 * ((j - 1) // 2))
                    sy.wait_ge(e2v, E2V(2 * j + 1, 3))
                    sy.dma_start(out=d_gateraw.ap()[j * 128:(j + 1) * 128, :],
                                 in_=GA2p[j % 2]).then_inc(g_gis, 16)
                if c >= 1:
                    sy.wait_ge(s_rru, 16 * c)
                if c >= 2:
                    sy.wait_ge(e2v, E2V(2 * c - 3, 3))
                    sy.wait_ge(e2s, E2S(2 * c - 3, 3))
                sy.dma_start(out=UP2p[c % 2],
                             in_=d_upraw.ap()[c * 128:(c + 1) * 128, :]
                             ).then_inc(s_rru, 16)
            sy.wait_ge(g_gis, 16 * 11)
            sy.wait_ge(e2v, E2V(2 * 23 + 1, 3))
            sy.dma_start(out=d_gateraw.ap()[23 * 128:24 * 128, :],
                         in_=GA2p[1]).then_inc(g_gis, 16)

            sy.wait_ge(g_gi, 16 * 12)
            sy.wait_ge(g_gis, 16 * 12)
            sy.wait_ge(e2v, 5 * NI2)
            for c in range(NI):
                if c >= 1:
                    sy.wait_ge(s_gil, 16 * c)
                if c >= 2:
                    sy.wait_ge(e3v, E3V(2 * c - 3, 1))
                sy.dma_start(out=GR3p[c % 2],
                             in_=d_gateraw.ap()[c * 128:(c + 1) * 128, :]
                             ).then_inc(s_gil, 16)
            sy.wait_ge(g_g2, 16 * NI)
            sy.wait_ge(e3v, 2 * NI2)
            sy.wait_ge(e_mm, NI)
            for tg in range(4):
                for c in range(NI):
                    gi = tg * NI + c
                    if gi >= 2:
                        sy.wait_ge(d_mm, gi - 1)
                    base = (gi % 2) * 4096
                    sy.dma_start(out=wbuf[:, base:base + 1024],
                                 in_=dg_out.ap()[c * 128:(c + 1) * 128, :]
                                 ).then_inc(s_wd[gi % 2], 16)

                for tt in range(4):
                    t = tg * 4 + tt
                    par = tt % 2
                    if t >= 2:
                        sy.wait_ge(f_ev, (t - 2) * 4 + 4)
                    sy.dma_start(out=xt[par][:, :],
                                 in_=x_sh.ap()[t * 128:(t + 1) * 128, :]
                                 ).then_inc(s_x[par], 16)
                    if t >= 1:
                        sy.wait_ge(f_ev, (t - 1) * 4 + 4)
                    sy.dma_start(out=f32b[:, :],
                                 in_=d_att.ap()[t * 128:(t + 1) * 128, :]
                                 ).then_inc(s_att, 16)

        @block.gpsimd
        def _(g):
            g.dma_start(out=idn[:, :], in_=ident_d.ap()).then_inc(g_idl, 16)
            g.dma_start(out=af[:, 0:3072].rearrange("p (t d) -> p t d", d=128),
                        in_=up_sl.ap().rearrange("(t p) d -> p t d", p=128)
                        ).then_inc(g_wsl[0], 16)
            g.dma_start(out=af[:, 3072:6144].rearrange("p (t d) -> p t d", d=128),
                        in_=gate_sl.ap().rearrange("(t p) d -> p t d", p=128)
                        ).then_inc(g_wsl[0], 16)
            g.dma_start(out=af[:, 6144:9216].rearrange("p (t d) -> p t d", d=384),
                        in_=down_sl.ap().rearrange("(t p) d -> p t d", p=128)
                        ).then_inc(g_wsl[0], 16)
            def xn_store(i):
                g.wait_ge(a_v, i * 5 + 4)
                g.dma_start(out=d_xn.ap()[i * 128:(i + 1) * 128, :],
                            in_=f32a[:, (i % 2) * 1024:(i % 2) * 1024 + 1024]
                            ).then_inc(g_xn, 16)

            for gi, widx, ti, nb, ncols, nsub, sb in WMETA:
                for s in range(nsub):
                    S = sb + s
                    g.wait_ge(w_e, S + 1)
                    if widx < 2:
                        dst = W_AGIN[widx].ap()[:, ti * 128:(ti + 1) * 128]
                    else:
                        dst = W_AGIN[2].ap()[s * 128:(s + 1) * 128,
                                             ti * 128:(ti + 1) * 128]
                    g.dma_start(out=dst, in_=tst[S % 2][:, :]
                                ).then_inc(g_wag[S % 2], 16)
                    if (S + 1) % 4 == 0 and (S + 1) // 4 - 1 < 12:
                        xn_store((S + 1) // 4 - 1)
                    if (S + 1) % 4 == 0 and (S + 1) // 4 - 1 < NDC:
                        jq = (S + 1) // 4 - 1
                        g.dma_start(out=wbuf[:, jq * 1024:(jq + 1) * 1024],
                                    in_=wT["q"].ap()[jq * 128:(jq + 1) * 128, :]
                                    ).then_inc(s_wq, 16)
            for i in range(12, NTK):
                xn_store(i)

            def ar_stage(k):
                mb, gb = 4 * k, 64 * k
                g.wait_ge(mrv, mb + 1)
                g.dma_start(out=d_pr[k].ap(), in_=red2[:, :]).then_inc(g_pr, 16)
                g.wait_ge(g_pr, gb + 16)
                g.dma_start(out=rowb[0:1, :],
                            in_=d_pr[k].ap().rearrange("p c -> (p c)").unsqueeze(0)
                            ).then_inc(g_pr, 16)
                g.wait_ge(mrv, mb + 2)
                g.dma_start(out=cc_in[k].ap(), in_=locm[:, :]
                            ).then_inc(g_pr, 16)
                g.wait_ge(g_pr, gb + 48)
                g.collective_compute("AllGather", ALU.bypass,
                                     replica_groups=[list(range(NCORE))],
                                     ins=[cc_in[k].ap().opt()],
                                     outs=[cc_out[k].ap().opt()]).then_inc(c_m[k])
                g.wait_ge(c_m[k], 1)
                g.dma_start(out=garow[0:1, :],
                            in_=cc_out[k].ap().rearrange("r c -> (r c)").unsqueeze(0)
                            ).then_inc(g_pr, 16)

            def attw_load(p):
                if p in ("k", "v", "o"):
                    g.wait_ge(p_mm, PROJ_BASE[p])
                for j in range(NDC):
                    g.dma_start(out=wbuf[:, j * 1024:(j + 1) * 1024],
                                in_=wT[p].ap()[j * 128:(j + 1) * 128, :]
                                ).then_inc(g_w, 16)

            ar_stage(0)
            g.wait_ge(g_wag[0], 16 * ((NWSUB + 1) // 2))
            g.wait_ge(g_wag[1], 16 * (NWSUB // 2))
            for ccs, cin, cout in ((c_w[0], ug_in, ug_out),
                                   (c_w[1], gg_in, gg_out),
                                   (c_w[2], dg_in, dg_out)):
                g.collective_compute("AllGather", ALU.bypass,
                                     replica_groups=[list(range(NCORE))],
                                     ins=[cin.ap().opt()], outs=[cout.ap().opt()]
                                     ).then_inc(ccs)
            for idx in range(NT):
                g.wait_ge(l2_v, (idx + 1) * 2)
                g.dma_start(out=d_q.ap()[idx * 128:(idx + 1) * 128, :],
                            in_=O_sb[:, :]).then_inc(g_qk, 16)
            attw_load("k")
            for ti in range(NTK):
                g.wait_ge(l2_v, (NT + ti + 1) * 2)
                g.dma_start(out=d_k.ap()[ti * 128:(ti + 1) * 128, :],
                            in_=O_sb[:, :]).then_inc(g_qk, 16)
            attw_load("v")
            attw_load("o")
            for T in range(NT):
                g.wait_ge(b_on, T + 1)
                g.dma_start(out=d_o.ap()[T * 128:(T + 1) * 128, :],
                            in_=O_sb[:, :]).then_inc(g_o, 16)
            for tc in range(NT):
                g.wait_ge(p_ev, PROJ_BASE["o"] + tc + 1)
                g.dma_start(out=d_att.ap()[tc * 128:(tc + 1) * 128, :],
                            in_=f32a[:, (tc % 2) * 1024:(tc % 2) * 1024 + 1024]
                            ).then_inc(g_attw, 16)
            for c in range(NI):
                g.wait_ge(e_v, c + 1)
                g.dma_start(out=d_upraw.ap()[c * 128:(c + 1) * 128, :],
                            in_=upst[c % 2]).then_inc(g_raw, 16)
                g.dma_start(out=d_gateraw.ap()[c * 128:(c + 1) * 128, :],
                            in_=gast[c % 2]).then_inc(g_raw, 16)
            ar_stage(1)
            g.wait_ge(g_raw, 16 * 2 * NI)
            for c in range(NI):
                if c >= 2 and (c - 2) % 2 == 0:
                    j = c - 2
                    g.wait_ge(g_gi, 16 * (j // 2))
                    g.wait_ge(e2v, E2V(2 * j + 1, 3))
                    g.dma_start(out=d_gateraw.ap()[j * 128:(j + 1) * 128, :],
                                in_=GA2p[j % 2]).then_inc(g_gi, 16)
                if c >= 1:
                    g.wait_ge(s_rrg, 16 * c)
                if c >= 2:
                    g.wait_ge(e2v, E2V(2 * c - 3, 4))
                    if c % 2 == 0:
                        g.wait_ge(g_gi, 16 * (c // 2))
                    else:
                        g.wait_ge(g_gis, 16 * ((c - 1) // 2))
                g.dma_start(out=GA2p[c % 2],
                            in_=d_gateraw.ap()[c * 128:(c + 1) * 128, :]
                            ).then_inc(s_rrg, 16)
            g.wait_ge(g_gi, 16 * 11)
            g.wait_ge(e2v, E2V(2 * 22 + 1, 3))
            g.dma_start(out=d_gateraw.ap()[22 * 128:23 * 128, :],
                        in_=GA2p[0]).then_inc(g_gi, 16)
            ar_stage(2)
            for c in range(NI):
                if c >= 1:
                    g.wait_ge(g_g2, 16 * c)
                g.wait_ge(e3s, E3S(2 * c + 1, 1))
                g.dma_start(out=d_g2int.ap()[c * 128:(c + 1) * 128, :],
                            in_=I163p[c % 2]).then_inc(g_g2, 16)
            ar_stage(3)
            g.wait_ge(g_g2, 16 * NI)
            g.wait_ge(e3v, 2 * NI2)
            for tg4 in range(4):
                for c4 in range(NI):
                    gi4 = tg4 * NI + c4
                    if gi4 >= 1:
                        g.wait_ge(s_gl[(gi4 - 1) % 2], 16 * ((gi4 - 1) // 2 + 1))
                    if gi4 >= 4:
                        g.wait_ge(e4v, E4V(gi4 - 4, 2))
                    g.dma_start(out=i16s[:, (gi4 % 4) * 512:(gi4 % 4) * 512 + 512],
                                in_=d_g2int.ap()[c4 * 128:(c4 + 1) * 128,
                                                 tg4 * 512:(tg4 + 1) * 512]
                                ).then_inc(s_gl[gi4 % 2], 16)
                for tt4 in range(4):
                    t = tg4 * 4 + tt4
                    g.wait_ge(f_ev, t * 4 + 4)
                    g.dma_start(out=out_sh.ap()[t * 128:(t + 1) * 128, :],
                                in_=f32a[:, 1024:2048]).then_inc(g_st, 16)
            g.wait_ge(g_st, 16 * NT)
            if debug:
                nst = 0
                for o, src in ((dbg["sc"], sc_row[:, :]), ):
                    g.dma_start(out=o.ap(), in_=src).then_inc(g_st, 16)
                    nst += 1
                for o, dsrc in ((dbg["att"], d_att), (dbg["xn"], d_xn),
                                (dbg["q"], d_q), (dbg["k"], d_k),
                                (dbg["ug"], ug_out), (dbg["dg"], dg_out)):
                    g.dma_start(out=o.ap(), in_=dsrc.ap()).then_inc(g_st, 16)
                    nst += 1
                g.dma_start(out=dbg["raw"].ap()[0:128, :],
                            in_=d_upraw.ap()[0:128, :]).then_inc(g_st, 16)
                g.dma_start(out=dbg["raw"].ap()[128:256, :],
                            in_=d_gateraw.ap()[0:128, :]).then_inc(g_st, 16)
                nst += 2
                g.wait_ge(g_st, 16 * NT + 16 * nst)

        @block.vector
        def _(v):
            v.memset(ones_f[:, :], 1.0)
            v.memset(epsb[:, :], 1e-5)
            v.memset(magp[:, :], MAGIC)
            v.memset(magn[:, :], -MAGIC)
            v.memset(red2[:, :], 0.0)
            v.memset(locm[:, :], 0.0)
            v.memset(sc_row[:, :], 0.0)
            WV6 = lambda c, k: zpos(6, c, k)

            def w_op(gi, k):
                _, widx, ti, nb, ncols, nsub, sb = WMETA[gi]
                p = gi % 2
                buf = af[:, widx * 3072 + ti * ncols:
                         widx * 3072 + (ti + 1) * ncols]
                wq, wf, we = wqs2[p], wfl2[p], weff2[p]
                if k == 0:
                    v.wait_ge(g_wsl[0], 16 * (widx + 1))
                    if gi >= 2:
                        v.wait_ge(w_v, WV6(gi - 2, 5))
                    v.tensor_reduce(wq[:, 0:nb],
                                    buf[:, 0:ncols].rearrange("p (b k) -> p b k",
                                                              k=32),
                                    AX.X, ALU.max,
                                    apply_absolute_value=True).then_inc(w_v, 1)
                elif k == 1:
                    v.wait_ge(w_v, WV6(gi, 0))
                    v.tensor_scalar(wq[:, 16:16 + nb], wq[:, 0:nb], INV_QW, 1e-12,
                                    ALU.mult, ALU.max).then_inc(w_v, 1)
                elif k == 2:
                    v.wait_ge(w_v, WV6(gi, 1))
                    v.reciprocal(wq[:, 32:32 + nb], wq[:, 16:16 + nb]
                                 ).then_inc(w_v, 1)
                elif k == 3:
                    v.wait_ge(w_v, WV6(gi, 2))
                    v.tensor_tensor(wf[:, 0:ncols].rearrange("p (b k) -> p b k",
                                                             k=32),
                                    buf[:, 0:ncols].rearrange("p (b k) -> p b k",
                                                              k=32),
                                    wq[:, 32:32 + nb].unsqueeze(2).broadcast_to(
                                        (128, nb, 32)),
                                    ALU.mult).then_inc(w_v, 1)
                elif k == 4:
                    v.wait_ge(w_v, WV6(gi, 3))
                    v.tensor_scalar(wf[:, 0:ncols], wf[:, 0:ncols], MAGIC, MAGIC,
                                    ALU.add, ALU.subtract).then_inc(w_v, 1)
                elif k == 5:
                    v.wait_ge(w_v, WV6(gi, 4))
                    if gi >= 2:
                        v.wait_ge(w_t, WMETA[gi - 2][6] + WMETA[gi - 2][5])
                    v.tensor_tensor(we[:, 0:ncols].rearrange("p (b k) -> p b k",
                                                             k=32),
                                    wf[:, 0:ncols].rearrange("p (b k) -> p b k",
                                                             k=32),
                                    wq[:, 16:16 + nb].unsqueeze(2).broadcast_to(
                                        (128, nb, 32)),
                                    ALU.mult).then_inc(w_v, 1)
                else:
                    for s in range(nsub):
                        S = sb + s
                        v.wait_ge(w_t, S + 1)
                        if S >= 2:
                            v.wait_ge(g_wag[S % 2], 16 * (S // 2))
                        psrc = (psp if S % 2 == 0 else pp[0])[:, 0:64].bitcast(BF16)
                        v.tensor_copy(tst[S % 2][:, :], psrc).then_inc(w_e, 1)

            def w_pair(cp):
                for k in range(7):
                    w_op(cp, k)
                    if cp + 1 < NWTL:
                        w_op(cp + 1, k)

            def a_tile(i):
                xb = xt[i % 2]
                v.wait_ge(s_x[i % 2], 16 * (i // 2 + 1))
                v.reduce_sum(stats[:, 0:1], xb[:, :], AX.X).then_inc(a_v, 1)
                v.wait_ge(a_s, i * 2 + 1)
                v.wait_ge(a_v, i * 5 + 1)
                vst(v.tensor_scalar(stats[:, 2:3], stats[:, 0:1], 1.0 / D, None,
                                    ALU.mult))
                v.tensor_scalar(stats[:, 3:4], stats[:, 1:2], 1.0 / D, None, ALU.mult)
                v.wait_ge(vs, vsn[0])
                vst(v.tensor_tensor(stats[:, 4:5], stats[:, 2:3], stats[:, 2:3],
                                    ALU.mult))
                v.wait_ge(vs, vsn[0])
                v.tensor_tensor(stats[:, 5:6], stats[:, 3:4], stats[:, 4:5],
                                ALU.subtract).then_inc(a_v, 1)
                v.wait_ge(a_s, i * 2 + 2)
                v.scalar_tensor_tensor(stats[:, 8:9], stats[:, 2:3], -1.0,
                                       stats[:, 7:8], ALU.mult, ALU.mult
                                       ).then_inc(a_v, 1)
                half = (i % 2) * 1024
                if i >= 2:
                    v.wait_ge(g_xn, 16 * (i - 1))
                v.wait_ge(a_v, i * 5 + 3)
                v.scalar_tensor_tensor(
                    f32a[:, half:half + 1024], xb[:, :], stats[:, 7:8],
                    stats[:, 8:9].broadcast_to((128, 1024)),
                    ALU.mult, ALU.add).then_inc(a_v, 1)
                v.wait_ge(a_v, i * 5 + 4)
                v.tensor_reduce(m0acc[:, i:i + 1], f32a[:, half:half + 1024],
                                AX.X, ALU.max, apply_absolute_value=True
                                ).then_inc(a_v, 1)

            for st2 in range(28):
                w_pair(2 * st2)
                if st2 % 2 == 0 and st2 // 2 < NTK:
                    a_tile(st2 // 2)
            for i in range(14, NTK):
                a_tile(i)

            def ar_vec(k, slots, accs, waits):
                mb, gb = 4 * k, 64 * k
                for sm, val in waits:
                    v.wait_ge(sm, val)
                for j, acc in enumerate(accs):
                    ins = v.tensor_reduce(red2[:, j:j + 1], acc, AX.X, ALU.max)
                    if j == len(accs) - 1:
                        ins.then_inc(mrv, 1)
                v.wait_ge(g_pr, gb + 32)
                v.tensor_reduce(locm[0:1, 0:2],
                                rowb[0:1, 0:256].rearrange("p (q c) -> p c q", c=2),
                                AX.X, ALU.max).then_inc(mrv, 1)
                v.wait_ge(g_pr, gb + 64)
                vst(v.tensor_reduce(locm[0:1, 0:2],
                                    garow[0:1, 0:64].rearrange("p (r c) -> p c r",
                                                               c=8)[:, 0:2, :],
                                    AX.X, ALU.max))
                last = None
                for j, (ssl, rsl) in enumerate(slots):
                    v.wait_ge(vs, vsn[0])
                    vst(v.tensor_scalar(sc_row[0:1, ssl:ssl + 1], locm[0:1, j:j + 1],
                                        INV_QA, 1e-12, ALU.mult, ALU.max))
                    v.wait_ge(vs, vsn[0])
                    last = v.reciprocal(sc_row[0:1, rsl:rsl + 1],
                                        sc_row[0:1, ssl:ssl + 1])
                    if k == 1 or (j, (ssl, rsl)) != (len(slots) - 1, slots[-1]):
                        vst(last)
                if k == 1:
                    v.wait_ge(vs, vsn[0])
                    last = v.tensor_scalar(sc_row[0:1, SGQ:SGQ + 1],
                                           sc_row[0:1, SG:SG + 1],
                                           1.0 / 65536.0, None, ALU.mult)
                last.then_inc(mrv, 1)
                v.wait_ge(scb, k + 1)
                v.tensor_copy(SC[:, :], psp[:, 0:16]).then_inc(mrv, 1)

            ar_vec(0, [(S0, R0)], [m0acc[:, :]], [(a_v, 5 * NTK)])

            for proj in ("q", "k", "v"):
                ntl = NT if proj == "q" else NTK
                for tc in range(ntl):
                    idx = PROJ_BASE[proj] + tc
                    par = tc % 2
                    bk = [pp[2 * par], pp[2 * par + 1]]
                    v.wait_ge(p_mm, idx + 1)
                    if proj in ("q", "k"):
                        qki = idx if proj == "q" else NT + tc
                        v.tensor_copy(f32a[:, 0:512], bk[0][:, :])
                        v.tensor_copy(f32a[:, 512:1024], bk[1][:, :]
                                      ).then_inc(p_ev, 1)
                        v.wait_ge(l2_s, qki * 2 + 1)
                        vst(v.tensor_reduce(ss[:, 0:16],
                                            f32b[:, :].rearrange("p (h e) -> p h e",
                                                                 e=64),
                                            AX.X, ALU.add))
                        v.wait_ge(vs, vsn[0])
                        v.tensor_scalar(ss[:, 0:16], ss[:, 0:16], 1e-12, None,
                                        ALU.max).then_inc(l2_v, 1)
                        v.wait_ge(l2_s, qki * 2 + 2)
                        v.wait_ge(l2_v, qki * 2 + 1)
                        vst(v.tensor_scalar(ss[:, 0:16], ss[:, 0:16], 1e6, None,
                                            ALU.min))
                        if qki >= 1:
                            v.wait_ge(g_qk, 16 * qki)
                        v.wait_ge(vs, vsn[0])
                        v.tensor_tensor(
                            O_sb[:, :].rearrange("p (h e) -> p h e", e=64),
                            f32a[:, 0:1024].rearrange("p (h e) -> p h e", e=64),
                            ss[:, 0:16].unsqueeze(2).broadcast_to((128, 16, 64)),
                            ALU.mult).then_inc(l2_v, 1)
                    else:
                        vt = tc
                        base = vt * NH * 65
                        v.tensor_copy(
                            V[:, base:base + 520].rearrange("p (h e) -> p h e",
                                                            e=65)[:, :, 0:64],
                            bk[0][:, :].rearrange("p (h e) -> p h e", e=64))
                        v.tensor_copy(
                            V[:, base + 520:base + 1040].rearrange(
                                "p (h e) -> p h e", e=65)[:, :, 0:64],
                            bk[1][:, :].rearrange("p (h e) -> p h e", e=64))
                        v.memset(V[:, base:base + 1040].rearrange(
                            "p (h e) -> p h e", e=65)[:, :, 64:65], 1.0
                            ).then_inc(p_ev, 1)

            v.wait_ge(s_e, 16 * NH * 3)
            for T in range(NT):
                for h in range(NH):
                    gx = T * NH + h
                    par = gx % 4
                    v.wait_ge(b_exp, gx + 1)
                    if gx >= 4:
                        v.wait_ge(b_av, gx - 3)
                    if T < NT - 1:
                        esl = E[:, h * 384:h * 384 + 256].rearrange(
                            "p (c q) -> p c q", q=128)
                    else:
                        esl = E[:, h * 384:h * 384 + 384].rearrange(
                            "p (c q) -> p c q", q=128)[:, 0:3:2, :]
                    v.tensor_tensor(PT[par][:, :].rearrange("p (c q) -> p c q", q=128),
                                    f32b[:, par * 256:(par + 1) * 256].rearrange(
                                        "p (c q) -> p c q", q=128),
                                    esl, ALU.mult).then_inc(b_pt, 1)
                v.wait_ge(b_av, (T + 1) * NH)
                if T == 0:
                    v.wait_ge(g_qk, 16 * (NT + NTK))
                if T >= 1:
                    v.wait_ge(g_o, 16 * T)
                v.reciprocal(
                    recs[:, 0:14].rearrange("p (b j) -> p b j", j=7),
                    pbO[:, 0:1024].rearrange("p (b q) -> p b q", q=512)[:, :, 64:512:65])
                vst(v.reciprocal(recs[:, 14:16], pbO[:, 1088:1154:65]))
                v.wait_ge(vs, vsn[0])
                for b in range(3):
                    nhh = 7 if b < 2 else 2
                    v.tensor_tensor(
                        O_sb[:, b * 448:b * 448 + nhh * 64].rearrange(
                            "p (h e) -> p h e", e=64),
                        pbO[:, b * 512:b * 512 + nhh * 65].rearrange(
                            "p (h e) -> p h e", e=65)[:, :, 0:64],
                        recs[:, b * 7:b * 7 + nhh].unsqueeze(2).broadcast_to(
                            (128, nhh, 64)),
                        ALU.mult).then_inc(b_on, 1 if b == 2 else 0)

            for j in range(NDC):
                v.wait_ge(xq_s, j + 1)
                v.tensor_scalar(ab[:, j * TOWN:(j + 1) * TOWN], Fal2,
                                MAGIC, SC[:, S0:S0 + 1], ALU.subtract,
                                ALU.mult).then_inc(xq_v, 1)
            for c in range(NI):
                v.wait_ge(e_ev, 8 * (c + 1))
                v.tensor_reduce(mupacc[:, c:c + 1], upst[c % 2], AX.X, ALU.max,
                                apply_absolute_value=True)
                v.tensor_reduce(mgacc[:, c:c + 1], gast[c % 2], AX.X, ALU.max,
                                apply_absolute_value=True).then_inc(e_v, 1)
            ar_vec(1, [(SUP, RUP), (SG, RG)], [mupacc[:, :], mgacc[:, :]],
                   [(e_v, NI)])
            def e2v_op(u, k):
                p = u % 4
                if k == 0:
                    v.wait_ge(e2s, E2S(u, 2))
                    v.tensor_scalar(UP2[p], T12[p], MAGIC, SC[:, SGQ:SGQ + 1],
                                    ALU.subtract, ALU.mult).then_inc(e2v, 1)
                elif k == 1:
                    v.wait_ge(e2s, E2S(u, 3))
                    v.wait_ge(e2v, E2V(u, 0))
                    v.tensor_scalar(T12[p], GA2[p], 65536.0, 65535.0, ALU.mult,
                                    ALU.min).then_inc(e2v, 1)
                elif k == 2:
                    v.wait_ge(e2v, E2V(u, 1))
                    v.tensor_scalar(T12[p], T12[p], MAGIC, None,
                                    ALU.add).then_inc(e2v, 1)
                elif k == 3:
                    v.wait_ge(e2v, E2V(u, 2))
                    v.scalar_tensor_tensor(GA2[p], T12[p], MAGIC, UP2[p],
                                           ALU.subtract, ALU.mult
                                           ).then_inc(e2v, 1)
                else:
                    v.wait_ge(e2v, E2V(u, 3))
                    v.tensor_reduce(mg2acc[:, u:u + 1], GA2[p], AX.X, ALU.max,
                                    apply_absolute_value=True).then_inc(e2v, 1)

            for q0 in range(0, NI2, 4):
                for k in range(5):
                    for u in range(q0, q0 + 4):
                        e2v_op(u, k)
            ar_vec(2, [(SG2, RG2)], [mg2acc[:, :]], [(e2v, 5 * NI2)])
            def e3v_op(u, k):
                p = u % 4
                if k == 0:
                    v.wait_ge(e3s, E3S(u, 2))
                    v.scalar_tensor_tensor(GR3[p], I163[p], SC[:, SG2:SG2 + 1],
                                           UQ3[p], ALU.mult, ALU.mult
                                           ).then_inc(e3v, 1)
                else:
                    v.wait_ge(e3v, E3V(u, 0))
                    v.tensor_reduce(moacc[:, u:u + 1], GR3[p], AX.X, ALU.max,
                                    apply_absolute_value=True).then_inc(e3v, 1)

            for q0 in range(0, NI2, 4):
                for k in range(2):
                    for u in range(q0, q0 + 4):
                        e3v_op(u, k)
            ar_vec(3, [(SO, RO)], [moacc[:, :]], [(e3v, 2 * NI2)])
            def e4v_op(tg, c, k):
                gi = tg * NI + c
                q = QA4[gi % 4]
                if k == 0:
                    v.wait_ge(e4s, E4S(gi, 1))
                    if gi >= 4:
                        v.wait_ge(e4v, E4V(gi - 4, 2))
                    v.tensor_tensor(q[:, 1024:1536], q[:, 0:512],
                                    q[:, 512:1024], ALU.mult).then_inc(e4v, 1)
                elif k == 1:
                    v.wait_ge(e4v, E4V(gi, 0))
                    v.tensor_scalar(q[:, 1536:2048], q[:, 1024:1536],
                                    SC[:, RO:RO + 1], MAGIC, ALU.mult, ALU.add
                                    ).then_inc(e4v, 1)
                else:
                    v.wait_ge(e4v, E4V(gi, 1))
                    if gi >= 4:
                        v.wait_ge(d_mm, gi - 3)
                    v.tensor_scalar(ot[gi % 4][:, :], q[:, 1536:2048],
                                    MAGIC, SC[:, SO:SO + 1],
                                    ALU.subtract, ALU.mult).then_inc(e4v, 1)

            for tg in range(4):
                for q0 in range(0, NI, 4):
                    for k in range(3):
                        for cc2 in range(q0, q0 + 4):
                            e4v_op(tg, cc2, k)
                for tt in range(4):
                    t = tg * 4 + tt
                    par = tt % 2
                    v.wait_ge(f_ev, 4 * t + 2)
                    nx = (AX0 if par == 0 else AX1) + tg * 2 + tt // 2 + 1
                    v.wait_ge(s_x[par], 16 * nx)
                    v.wait_ge(s_att, 16 * (t + 1))
                    if t >= 1:
                        v.wait_ge(g_st, 16 * t)
                    v.tensor_tensor(f32a[:, 1024:2048], xt[par][:, :], f32b[:, :],
                                    ALU.add).then_inc(f_ev, 1)
                    v.wait_ge(f_ev, 4 * t + 3)
                    v.tensor_tensor(f32a[:, 1024:2048], f32a[:, 1024:2048],
                                    f32a[:, 0:1024], ALU.add).then_inc(f_ev, 1)

        @block.scalar
        def _(sc):
            for i in range(NTK):
                xb = xt[i % 2]
                sc.wait_ge(a_v, 5 * i + 1)
                sc.activation(f32b[:, :], xb[:, :], AF.Square,
                              accum_out=stats[:, 1:2]).then_inc(a_s, 1)
                sc.wait_ge(a_v, 5 * i + 2)
                sst(sc.activation(stats[:, 6:7], stats[:, 5:6], AF.Ln,
                                  bias=epsb[:, 0:1]))
                sc.wait_ge(ssf, ssn[0])
                sc.activation(stats[:, 7:8], stats[:, 6:7], AF.Exp,
                              scale=-0.5).then_inc(a_s, 1)
            for proj in ("q", "k"):
                ntl = NT if proj == "q" else NTK
                for tc in range(ntl):
                    qki = tc if proj == "q" else NT + tc
                    idx = PROJ_BASE[proj] + tc
                    sc.wait_ge(p_ev, idx + 1)
                    sc.activation(f32b[:, :], f32a[:, 0:1024], AF.Square
                                  ).then_inc(l2_s, 1)
                    sc.wait_ge(l2_v, qki * 2 + 1)
                    sst(sc.activation(ss[:, 16:32], ss[:, 0:16], AF.Ln))
                    sc.wait_ge(ssf, ssn[0])
                    sc.activation(ss[:, 0:16], ss[:, 16:32], AF.Exp,
                                  scale=-0.5).then_inc(l2_s, 1)
            for T in range(NT):
                for h in range(NH):
                    gx = T * NH + h
                    par = gx % 4
                    sc.wait_ge(b_mm, gx + 1)
                    if gx >= 3:
                        sc.wait_ge(b_pt, gx - 2)
                    sc.activation(f32b[:, par * 256:(par + 1) * 256],
                                  pp[par][:, 0:256], AF.Exp).then_inc(b_exp, 1)
            for tc in range(NT):
                idx = PROJ_BASE["o"] + tc
                par = tc % 2
                sc.wait_ge(p_mm, idx + 1)
                if tc >= 2:
                    sc.wait_ge(g_attw, 16 * (tc - 1))
                sc.copy(f32a[:, par * 1024:par * 1024 + 512], pp[2 * par][:, :])
                sc.copy(f32a[:, par * 1024 + 512:par * 1024 + 1024],
                        pp[2 * par + 1][:, :]).then_inc(p_ev, 1)
            sc.wait_ge(p_mm, PROJ_BASE["o"] + NT)
            sc.wait_ge(b_pt, NT * NH)
            sc.wait_ge(mrv, 4)
            for j in range(NDC):
                if j >= 1:
                    sc.wait_ge(xq_v, j)
                sc.activation(Fal2, ab[:, XNc + j * TKV: XNc + j * TKV + 2048],
                              AF.Copy, bias=MAGIC,
                              scale=SC[:, R0:R0 + 1]).then_inc(xq_s, 1)
            sc.wait_ge(g_attw, 16 * NT)
            sc.wait_ge(b_pt, NT * NH)
            for c in range(NI):
                sc.wait_ge(e_mm, c + 1)
                if c >= 2:
                    sc.wait_ge(g_raw, 32 * (c - 1))
                    sc.wait_ge(e_v, c - 1)
                ub, gb2 = upst[c % 2], gast[c % 2]
                for tsub in range(4):
                    sc.copy(ub[:, tsub * 512:(tsub + 1) * 512], pp[tsub][:, :]
                            ).then_inc(e_ev, 1)
                for tsub in range(3):
                    sc.copy(gb2[:, tsub * 512:(tsub + 1) * 512],
                            pbO[:, tsub * 512:(tsub + 1) * 512]).then_inc(e_ev, 1)
                sc.copy(gb2[:, 1536:2048], psp[:, :]).then_inc(e_ev, 1)
            sc.wait_ge(mrv, 8)

            def e2s_op(u, k):
                p = u % 4
                if k == 0:
                    sc.wait_ge(s_rru, 16 * (u // 2 + 1))
                    if u >= 4:
                        sc.wait_ge(e2s, E2S(u - 4, 1))
                    sc.activation(T12a[p], UP2[p], AF.Copy, bias=MAGIC,
                                  scale=SC[:, RUP:RUP + 1]).then_inc(e2s, 1)
                elif k == 1:
                    if u == 0:
                        sc.wait_ge(e_mm, NI)
                    sc.wait_ge(e2s, E2S(u, 0))
                    sc.activation(arena[:, u * 1024:(u + 1) * 1024], T12a[p],
                                  AF.Copy, bias=-MAGIC).then_inc(e2s, 1)
                elif k == 2:
                    sc.wait_ge(s_rrg, 16 * (u // 2 + 1))
                    if u >= 4:
                        sc.wait_ge(e2v, E2V(u - 4, 3))
                    sc.activation(T12[p], GA2[p], AF.Copy, bias=MAGIC,
                                  scale=SC[:, RG:RG + 1]).then_inc(e2s, 1)
                else:
                    sc.wait_ge(e2v, E2V(u, 0))
                    sc.wait_ge(e2s, E2S(u, 2))
                    sc.activation(GA2[p], UP2[p], AF.Sigmoid,
                                  scale=65536.0).then_inc(e2s, 1)

            for q0 in range(0, NI2, 4):
                for k in range(4):
                    for u in range(q0, q0 + 4):
                        e2s_op(u, k)

            sc.wait_ge(mrv, 12)

            def e3s_op(u, k):
                p = u % 4
                if k == 0:
                    sc.wait_ge(s_gil, 16 * (u // 2 + 1))
                    if u >= 4:
                        sc.wait_ge(e3s, E3S(u - 4, 1))
                    sc.activation(T13[p], GR3[p], AF.Copy, bias=MAGIC,
                                  scale=SC[:, RG2:RG2 + 1]).then_inc(e3s, 1)
                elif k == 1:
                    sc.wait_ge(e3s, E3S(u, 0))
                    if u >= 4:
                        sc.wait_ge(g_g2, 16 * (u // 2 - 1))
                        sc.wait_ge(e3v, E3V(u - 4, 0))
                    sc.activation(I163[p], T13[p], AF.Copy,
                                  bias=-MAGIC).then_inc(e3s, 1)
                else:
                    sc.wait_ge(e3s, E3S(u, 0))
                    if u >= 4:
                        sc.wait_ge(e3v, E3V(u - 4, 0))
                    sc.activation(UQ3[p], arena[:, u * 1024:(u + 1) * 1024],
                                  AF.Copy, scale=SC[:, SUP:SUP + 1]
                                  ).then_inc(e3s, 1)

            for q0 in range(0, NI2, 4):
                for k in range(3):
                    for u in range(q0, q0 + 4):
                        e3s_op(u, k)

            sc.wait_ge(mrv, 16)

            def e4s_op(tg, c, k):
                gi = tg * NI + c
                q = QA4[gi % 4]
                if k == 0:
                    sc.wait_ge(s_gl[gi % 2], 16 * (gi // 2 + 1))
                    if gi >= 4:
                        sc.wait_ge(e4v, E4V(gi - 4, 0))
                    sc.activation(q[:, 0:512],
                                  i16s[:, (gi % 4) * 512:(gi % 4) * 512 + 512],
                                  AF.Copy, scale=SC[:, SG2:SG2 + 1]).then_inc(e4s, 1)
                else:
                    sc.activation(q[:, 512:1024],
                                  arena[:, c * 2048 + tg * 512:
                                        c * 2048 + (tg + 1) * 512],
                                  AF.Copy, scale=SC[:, SUP:SUP + 1]).then_inc(e4s, 1)

            for tg in range(4):
                for q0 in range(0, NI, 4):
                    for k in range(2):
                        for cc2 in range(q0, q0 + 4):
                            e4s_op(tg, cc2, k)
                for tt in range(4):
                    t = tg * 4 + tt
                    sc.wait_ge(d_mm, NI * (tg + 1))
                    if t >= 1:
                        sc.wait_ge(f_ev, 4 * t)
                    sc.copy(f32a[:, 0:512], e4bank(tt, 0)).then_inc(f_ev, 1)
                    sc.copy(f32a[:, 512:1024], e4bank(tt, 1)).then_inc(f_ev, 1)

        @block.tensor
        def _(t):
            t.wait_ge(g_idl, 16)
            for gi, widx, ti, nb, ncols, nsub, sb in WMETA:
                t.wait_ge(w_v, zpos(6, gi, 5))
                we = weff2[gi % 2]
                for s in range(nsub):
                    S = sb + s
                    if S >= 2:
                        t.wait_ge(w_e, S - 1)
                    dst = (psp if S % 2 == 0 else pp[0])[:, 0:64].bitcast(BF16)
                    t.transpose(dst, we[:, s * 128:(s + 1) * 128],
                                idn[:, :]).then_inc(w_t, 1)
            t.wait_ge(w_e, NWSUB)
            t.wait_ge(mrv, 3)
            t.matmul(psp[:, 0:16], ones_f[0:1, 0:128], sc_row[0:1, 0:16],
                     start=True, stop=True).then_inc(scb, 1)
            for proj in ("q", "k", "v"):
                ntl = NT if proj == "q" else NTK
                if proj != "q":
                    t.wait_ge(g_w, GW_P[proj])
                    t.wait_ge(s_tr, 16 * 16)
                for tc in range(ntl):
                    idx = PROJ_BASE[proj] + tc
                    par = tc % 2
                    if idx >= 2:
                        t.wait_ge(p_ev, idx - 1)
                    if proj == "q" and tc == 8:
                        t.wait_ge(s_tr, 16 * 16)
                    for j in range(NDC):
                        if proj == "q" and tc == 0:
                            t.wait_ge(s_tr, 16 * (j + 1))
                            t.wait_ge(s_wq, 16 * (j + 1))
                        lhsT = ab[:, XNc + j * TKV + tc * 128:
                                  XNc + j * TKV + (tc + 1) * 128]
                        ins = t.matmul(pp[2 * par][:, :], lhsT,
                                       wbuf[:, j * 1024:j * 1024 + 512],
                                       start=(j == 0), stop=(j == NDC - 1))
                        ins2 = t.matmul(pp[2 * par + 1][:, :], lhsT,
                                        wbuf[:, j * 1024 + 512:(j + 1) * 1024],
                                        start=(j == 0), stop=(j == NDC - 1))
                        if j == NDC - 1:
                            ins2.then_inc(p_mm, 1)
            t.wait_ge(s_tr, 16 * 32)
            t.wait_ge(p_ev, PROJ_BASE["o"])

            def scores(T, h):
                gx = T * NH + h
                par = gx % 4
                if gx >= 4:
                    t.wait_ge(b_exp, gx - 3)
                j = h // 2
                r0 = (h % 2) * 64
                for c in range(2):
                    lhsT = ab[r0:r0 + 64,
                              KTc + j * TKV + T * 128 + c * 128:
                              KTc + j * TKV + T * 128 + (c + 1) * 128]
                    rhs = ab[r0:r0 + 64, j * TOWN + T * 128: j * TOWN + (T + 1) * 128]
                    ins = t.matmul(pp[par][:, c * 128:(c + 1) * 128], lhsT, rhs,
                                   start=True, stop=True)
                    if c == 1:
                        ins.then_inc(b_mm, 1)

            def av(T, h):
                gx = T * NH + h
                par = gx % 4
                t.wait_ge(b_pt, gx + 1)
                if h == 0:
                    t.wait_ge(b_on, T)
                ocol = (h // 7) * 512 + (h % 7) * 65
                for c in range(2):
                    vbase = (T + c) * NH * 65 + h * 65
                    ins = t.matmul(pbO[:, ocol:ocol + 65],
                                   PT[par][:, c * 128:(c + 1) * 128],
                                   V[:, vbase:vbase + 65],
                                   start=(c == 0), stop=(c == 1))
                    if c == 1:
                        ins.then_inc(b_av, 1)

            for T in range(NT):
                scores(T, 0)
                for h in range(1, NH):
                    scores(T, h)
                    av(T, h - 1)
                av(T, NH - 1)
            t.wait_ge(s_tr, 16 * 40)
            t.wait_ge(g_w, GW_P["o"])
            t.wait_ge(b_exp, NT * NH)
            for tc in range(NT):
                idx = PROJ_BASE["o"] + tc
                par = tc % 2
                if tc >= 2:
                    t.wait_ge(p_ev, idx - 1)
                for j in range(NDC):
                    lhsT = ab[:, j * TOWN + tc * 128: j * TOWN + (tc + 1) * 128]
                    ins = t.matmul(pp[2 * par][:, :], lhsT,
                                   wbuf[:, j * 1024:j * 1024 + 512],
                                   start=(j == 0), stop=(j == NDC - 1))
                    ins2 = t.matmul(pp[2 * par + 1][:, :], lhsT,
                                    wbuf[:, j * 1024 + 512:(j + 1) * 1024],
                                    start=(j == 0), stop=(j == NDC - 1))
                    if j == NDC - 1:
                        ins2.then_inc(p_mm, 1)
            t.wait_ge(xq_v, NDC)
            for c in range(NI):
                t.wait_ge(s_wg[c % 2], 32 * (c // 2 + 1))
                base = (c % 2) * 4096
                for j in range(NDC):
                    for tsub in range(4):
                        if j == 0 and c >= 1:
                            t.wait_ge(e_ev, 8 * (c - 1) + tsub + 1)
                        rhs = ab[:, j * TOWN + tsub * 512: j * TOWN + (tsub + 1) * 512]
                        t.matmul(pp[tsub][:, :], wbuf[:, base + j * 128:
                                                      base + (j + 1) * 128],
                                 rhs, start=(j == 0), stop=(j == NDC - 1))
                    for tsub in range(4):
                        if j == 0 and c >= 1:
                            t.wait_ge(e_ev, 8 * (c - 1) + 4 + tsub + 1)
                        rhs = ab[:, j * TOWN + tsub * 512: j * TOWN + (tsub + 1) * 512]
                        gb = (pbO[:, tsub * 512:(tsub + 1) * 512] if tsub < 3
                              else psp[:, :])
                        ins = t.matmul(gb, wbuf[:, base + 1024 + j * 128:
                                                base + 1024 + (j + 1) * 128],
                                       rhs, start=(j == 0), stop=(j == NDC - 1))
                        if j == NDC - 1 and tsub == 3:
                            ins.then_inc(e_mm, 1)
            for k in (1, 2, 3):
                t.wait_ge(mrv, 4 * k + 3)
                t.matmul(psp[:, 0:16], ones_f[0:1, 0:128], sc_row[0:1, 0:16],
                         start=True, stop=True).then_inc(scb, 1)
            for tg in range(4):
                for c in range(NI):
                    gi = tg * NI + c
                    t.wait_ge(s_wd[gi % 2], 16 * (gi // 2 + 1))
                    t.wait_ge(e4v, E4V(gi, 2))
                    if c == 0 and tg >= 1:
                        t.wait_ge(f_ev, 16 * tg)
                    base = (gi % 2) * 4096
                    for s in range(4):
                        for hf in range(2):
                            ins = t.matmul(e4bank(s, hf),
                                           ot[gi % 4][:, s * 128:(s + 1) * 128],
                                           wbuf[:, base + hf * 512:
                                                base + (hf + 1) * 512],
                                           start=(c == 0), stop=(c == NI - 1))
                            if s == 3 and hf == 1:
                                ins.then_inc(d_mm, 1)

    return nc


def _make_etab(dt, half):
    base = np.linspace(0.0, 1.0, NH, dtype=dt)
    slopes = np.power(np.asarray(2.0, dtype=dt), (-8.0 * base).astype(dt))
    k = np.arange(128)
    q = np.arange(128)
    E = np.zeros((NH, 3, 128, 128), dtype=dt)
    for h in range(NH):
        for slot in range(3):
            c = 1 if slot >= 1 else 0
            rel = (c * 128 + k[:, None]) - q[None, :]
            valid = (rel >= 0) & (rel < 128)
            if slot == 2 and half == 1:
                valid = np.zeros_like(valid)
            vals = np.exp((-rel * slopes[h]).astype(dt)).astype(dt)
            E[h, slot] = np.where(valid, vals, np.asarray(0.0, dtype=dt))
    return np.ascontiguousarray(E.reshape(NH * 3 * 128, 128))


def make_inputs(L0_attn_x, L0_attn_Wq, L0_attn_Wk, L0_attn_Wv, L0_attn_Wo,
                L0_ffn_w_gate, L0_ffn_w_up, L0_ffn_w_down):
    x = np.asarray(L0_attn_x)
    dt = x.dtype
    wq = np.ascontiguousarray(np.asarray(L0_attn_Wq).T)
    wk = np.ascontiguousarray(np.asarray(L0_attn_Wk).T)
    wv = np.ascontiguousarray(np.asarray(L0_attn_Wv).T)
    wo = np.ascontiguousarray(np.asarray(L0_attn_Wo).T)
    wup = np.asarray(L0_ffn_w_up)
    wgate = np.asarray(L0_ffn_w_gate)
    wdown = np.asarray(L0_ffn_w_down)
    etabs = [_make_etab(dt, half) for half in range(2)]
    in_maps = []
    for c in range(NCORE):
        b, half = c // 2, c % 2
        xs = np.zeros((TKV, D), dtype=dt)
        xs[0:TOWN] = x[b, half * TOWN:(half + 1) * TOWN]
        if half == 0:
            xs[TOWN:TKV] = x[b, TOWN:TOWN + 128]
        in_maps.append(dict(
            x_sh=np.ascontiguousarray(xs),
            wqT=wq, wkT=wk, wvT=wv, woT=wo,
            up_sl=np.ascontiguousarray(wup[:, c * 128:(c + 1) * 128]),
            gate_sl=np.ascontiguousarray(wgate[:, c * 128:(c + 1) * 128]),
            down_sl=np.ascontiguousarray(wdown[:, c * 384:(c + 1) * 384]),
            etab=etabs[half],
            ident_d=np.eye(128, dtype=np.float32),
        ))
    return in_maps


def kernel(**inputs):
    in_maps = make_inputs(**inputs)
    nc = build_program(debug=False)
    res = run_bass_kernel_spmd(nc, in_maps, core_ids=list(range(NCORE)))
    out = np.empty((4, 4096, D), dtype=np.float32)
    for c in range(NCORE):
        b, half = c // 2, c % 2
        out[b, half * TOWN:(half + 1) * TOWN] = res.results[c]["out_sh"]
    return out


# version 24
# speedup vs baseline: 1.0830x; 1.0830x over previous
import time
import numpy as np
import concourse.bass as bass
import concourse.mybir as mybir
from concourse.bass_utils import run_bass_kernel_spmd

F32 = mybir.dt.float32
BF16 = mybir.dt.bfloat16
I16 = mybir.dt.int16
I32 = mybir.dt.int32
AF = mybir.ActivationFunctionType
ALU = mybir.AluOpType
AX = mybir.AxisListType

NCORE = 8
D = 1024
TOWN = 2048
TKV = 2176
NT = 16
NTK = 17
NH = 16
II = 3072
NI = 24
NDC = 8
INV_QA = 1.0 / 32767.0
INV_QW = 1.0 / 7.0
MAGIC = 12582912.0

S0, R0, SUP, RUP, SG, RG, SG2, RG2, SO, RO, SGQ = range(11)
WTILES = [(NI, 4, 1), (NI, 4, 1), (NDC, 12, 3)]
NWTL = NI + NI + NDC
NWSUB = NI + NI + NDC * 3


def _wmeta():
    g, sb = 0, 0
    out = []
    for widx, (ntl, nb, nsub) in enumerate(WTILES):
        for ti in range(ntl):
            out.append((g, widx, ti, nb, nb * 32, nsub, sb))
            g += 1
            sb += nsub
    return out


WMETA = _wmeta()


def build_program(debug=False):
    nc = bass.Bass("TRN2", target_bir_lowering=False,
                   detect_race_conditions=False,
                   name=f"mk_{time.time_ns() % 1000000000}")

    x_sh = nc.dram_tensor("x_sh", [TKV, D], F32, kind="ExternalInput")
    wT = {k: nc.dram_tensor(f"w{k}T", [D, D], F32, kind="ExternalInput")
          for k in "qkvo"}
    up_sl = nc.dram_tensor("up_sl", [II, 128], F32, kind="ExternalInput")
    gate_sl = nc.dram_tensor("gate_sl", [II, 128], F32, kind="ExternalInput")
    down_sl = nc.dram_tensor("down_sl", [D, 384], F32, kind="ExternalInput")
    etab = nc.dram_tensor("etab", [NH * 3 * 128, 128], F32, kind="ExternalInput")
    ident_d = nc.dram_tensor("ident_d", [128, 128], F32, kind="ExternalInput")

    out_sh = nc.dram_tensor("out_sh", [TOWN, D], F32, kind="ExternalOutput")
    dbg = {}
    if debug:
        dbg["sc"] = nc.dram_tensor("dbg_sc", [1, 16], F32, kind="ExternalOutput")
        dbg["att"] = nc.dram_tensor("dbg_att", [TOWN, D], F32, kind="ExternalOutput")
        dbg["xn"] = nc.dram_tensor("dbg_xn", [TKV, D], BF16, kind="ExternalOutput")
        dbg["q"] = nc.dram_tensor("dbg_q", [TOWN, D], BF16, kind="ExternalOutput")
        dbg["k"] = nc.dram_tensor("dbg_k", [TKV, D], BF16, kind="ExternalOutput")
        dbg["ug"] = nc.dram_tensor("dbg_ug", [D, II], BF16, kind="ExternalOutput")
        dbg["dg"] = nc.dram_tensor("dbg_dg", [II, D], BF16, kind="ExternalOutput")
        dbg["raw"] = nc.dram_tensor("dbg_raw", [256, TOWN], F32, kind="ExternalOutput")

    d_xn = nc.dram_tensor("d_xn", [TKV, D], BF16)
    d_q = nc.dram_tensor("d_q", [TOWN, D], BF16)
    d_k = nc.dram_tensor("d_k", [TKV, D], BF16)
    d_o = nc.dram_tensor("d_o", [TOWN, D], BF16)
    d_att = nc.dram_tensor("d_att", [TOWN, D], F32)
    dkind = dict(kind="ExternalOutput") if debug else {}
    d_upraw = nc.dram_tensor("d_upraw", [II, TOWN], F32, **dkind)
    d_gateraw = nc.dram_tensor("d_gateraw", [II, TOWN], F32, **dkind)
    d_gint = nc.dram_tensor("d_gint", [II, TOWN], I16)
    d_g2int = nc.dram_tensor("d_g2int", [II, TOWN], I16, **dkind)
    d_pr = [nc.dram_tensor(f"d_pr{i}", [128, 2], F32) for i in range(4)]
    cc_in = [nc.dram_tensor(f"cc_in{i}", [1, 8], F32) for i in range(4)]
    cc_out = [nc.dram_tensor(f"cc_out{i}", [NCORE, 8], F32, addr_space="Shared")
              for i in range(4)]
    ug_in = nc.dram_tensor("ug_in", [128, II], BF16)
    ug_out = nc.dram_tensor("ug_out", [D, II], BF16, addr_space="Shared")
    gg_in = nc.dram_tensor("gg_in", [128, II], BF16)
    gg_out = nc.dram_tensor("gg_out", [D, II], BF16, addr_space="Shared")
    dg_in = nc.dram_tensor("dg_in", [384, D], BF16)
    dg_out = nc.dram_tensor("dg_out", [II, D], BF16, addr_space="Shared")
    W_DRAM = [up_sl, gate_sl, down_sl]
    W_AGIN = [ug_in, gg_in, dg_in]

    from contextlib import ExitStack
    with ExitStack() as ctx:
        sbt = lambda n, s, d: ctx.enter_context(nc.sbuf_tensor(n, s, d))
        pst = lambda n, s: ctx.enter_context(nc.psum_tensor(n, s, F32))
        sem = lambda n: ctx.enter_context(nc.semaphore(n))

        E = sbt("E", [128, NH * 3 * 128], F32)
        arena = sbt("arena", [128, 51200], I16)
        ab = arena[:, :].bitcast(BF16)
        V = sbt("V", [128, NTK * NH * 65], BF16)
        wbuf = sbt("wbuf", [128, 8192], BF16)
        xt = [sbt("xt0", [128, D], F32), sbt("xt1", [128, D], F32)]
        f32a = sbt("f32a", [128, 2048], F32)
        f32b = sbt("f32b", [128, 1024], F32)
        i16s = sbt("i16s", [128, 2048], I16)
        ot = [sbt("ot0", [128, 512], BF16), sbt("ot1", [128, 512], BF16),
              None, None]
        O_sb = sbt("O_sb", [128, 1024], BF16)
        PT = [sbt("PT0", [128, 256], BF16), sbt("PT1", [128, 256], BF16),
              sbt("PT2", [128, 256], BF16), sbt("PT3", [128, 256], BF16)]
        stats = sbt("stats", [128, 16], F32)
        m0acc = sbt("m0acc", [128, NTK], F32)
        mupacc = sbt("mupacc", [128, NI], F32)
        mgacc = sbt("mgacc", [128, NI], F32)
        mg2acc = sbt("mg2acc", [128, 2 * NI], F32)
        moacc = sbt("moacc", [128, 2 * NI], F32)
        red2 = sbt("red2", [128, 2], F32)
        rowb = sbt("rowb", [1, 256], F32)
        garow = sbt("garow", [1, 64], F32)
        locm = sbt("locm", [1, 8], F32)
        sc_row = sbt("sc_row", [1, 16], F32)
        SC = sbt("SC", [128, 16], F32)
        ones_f = sbt("ones_f", [1, 128], F32)
        epsb = sbt("epsb", [128, 1], F32)
        ss = sbt("ss", [128, 32], F32)
        recs = sbt("recs", [128, 32], F32)
        idn = sbt("idn", [128, 128], BF16)
        magp = sbt("magp", [128, 1], F32)
        magn = sbt("magn", [128, 1], F32)
        nc.const_aps.register(F32, MAGIC, magp[:, 0:1])
        nc.const_aps.register(F32, -MAGIC, magn[:, 0:1])
        af = arena[:, :].bitcast(F32)
        wfl2 = [af[:, 9216:9600], af[:, 9600:9984]]
        wqs2 = [af[:, 9984:10032], af[:, 10032:10080]]
        weff2 = [O_sb[:, 0:384], O_sb[:, 384:768]]
        tst = [PT[0][:, 0:128], PT[1][:, 0:128]]
        Eal = E[:, 0:2048]
        Fal2 = E[:, 2048:4096]
        Vf = V[:, 0:12288].bitcast(F32)
        Vi = V[:, :].bitcast(I16)
        upst = [f32a[:, :], Vf[:, 0:2048]]
        gast = [Eal, Vf[:, 2048:4096]]
        UP2 = [f32a[:, 0:1024], f32a[:, 1024:2048],
               Vf[:, 0:1024], Vf[:, 1024:2048]]
        GA2 = [Eal[:, 0:1024], Eal[:, 1024:2048],
               Vf[:, 2048:3072], Vf[:, 3072:4096]]
        T12 = [Fal2[:, 0:1024], Fal2[:, 1024:2048],
               Vf[:, 4096:5120], Vf[:, 5120:6144]]
        wbf = wbuf[:, :].bitcast(F32)
        T12a = [wbf[:, 0:1024], wbf[:, 1024:2048],
                wbf[:, 2048:3072], wbf[:, 3072:4096]]
        GR3 = [Eal[:, 0:1024], Eal[:, 1024:2048],
               Vf[:, 0:1024], Vf[:, 1024:2048]]
        T13 = [Fal2[:, 0:1024], Fal2[:, 1024:2048],
               Vf[:, 2048:3072], Vf[:, 3072:4096]]
        UQ3 = [f32a[:, 0:1024], f32a[:, 1024:2048],
               Vf[:, 4096:5120], Vf[:, 5120:6144]]
        I163 = [i16s[:, 0:1024], i16s[:, 1024:2048],
                Vi[:, 12288:13312], Vi[:, 13312:14336]]
        UP2p = [f32a[:, 0:2048], Vf[:, 0:2048]]
        GA2p = [Eal, Vf[:, 2048:4096]]
        GR3p = [Eal, Vf[:, 0:2048]]
        I163p = [i16s[:, 0:2048], Vi[:, 12288:14336]]
        QA4 = [Eal, Fal2, Vf[:, 0:2048], Vf[:, 2048:4096]]
        ot[2] = O_sb[:, 0:512]
        ot[3] = O_sb[:, 512:1024]

        XNc = 33792
        KTc = 16384

        pp = [pst(f"pp{i}", [128, 512]) for i in range(4)]
        pbO = pst("pbO", [128, 1536])
        psp = pst("psp", [128, 512])

        def e4bank(s, hf):
            k = s * 2 + hf
            if k < 4:
                return pp[k][:, :]
            if k < 7:
                return pbO[:, (k - 4) * 512:(k - 3) * 512]
            return psp[:, :]

        s_x = [sem("s_x0"), sem("s_x1")]
        g_idl = sem("g_idl")
        g_wsl = [sem("g_wsl0"), sem("g_wsl1")]
        s_e = sem("s_e")
        s_tr = sem("s_tr")
        s_wq = sem("s_wq")
        g_gis = sem("g_gis")
        s_wg = [sem("s_wg0"), sem("s_wg1")]
        s_wd = [sem("s_wd0"), sem("s_wd1")]
        s_gl = [sem("s_gl0"), sem("s_gl1"), sem("s_gl2"), sem("s_gl3")]
        s_rru = sem("s_rru")
        s_rrg = sem("s_rrg")
        s_gil = sem("s_gil")
        s_att = sem("s_att")
        g_w = sem("g_w")
        g_wag = [sem("g_wag0"), sem("g_wag1")]
        g_xn = sem("g_xn")
        g_qk = sem("g_qk")
        g_o = sem("g_o")
        g_attw = sem("g_attw")
        g_raw = sem("g_raw")
        g_gi = sem("g_gi")
        g_g2 = sem("g_g2")
        g_st = sem("g_st")
        g_pr = sem("g_pr")
        c_w = [sem(f"c_w{i}") for i in range(3)]
        c_m = [sem(f"c_m{i}") for i in range(4)]
        w_v = sem("w_v")
        w_t = sem("w_t")
        w_e = sem("w_e")
        a_v = sem("a_v")
        a_s = sem("a_s")
        p_mm = sem("p_mm")
        p_ev = sem("p_ev")
        l2_v = sem("l2_v")
        l2_s = sem("l2_s")
        b_mm = sem("b_mm")
        b_exp = sem("b_exp")
        b_pt = sem("b_pt")
        b_av = sem("b_av")
        b_on = sem("b_on")
        xq_v = sem("xq_v")
        xq_s = sem("xq_s")
        e_mm = sem("e_mm")
        e_ev = sem("e_ev")
        e_v = sem("e_v")
        e2s = sem("e2s")
        e2v = sem("e2v")
        e3s = sem("e3s")
        e3v = sem("e3v")
        e4s = sem("e4s")
        e4v = sem("e4v")
        d_mm = sem("d_mm")
        f_ev = sem("f_ev")
        mrv = sem("mrv")
        scb = sem("scb")
        vs = sem("vs")
        ssf = sem("ssf")
        vsn = [0]
        ssn = [0]

        def vst(ins):
            vsn[0] += 1
            ins.then_inc(vs, 1)

        def sst(ins):
            ssn[0] += 1
            ins.then_inc(ssf, 1)

        GW_P = {p: 128 * (i + 1) for i, p in enumerate("kvo")}

        def zpos(npc, c, k):
            p = c % 2
            return npc * (c - p) + 2 * k + p + 1
        NI2 = 2 * NI
        zq4 = lambda npc, u, k: npc * (u - u % 4) + 4 * k + (u % 4) + 1
        E2V = lambda u, k: zq4(5, u, k)
        E2S = lambda u, k: zq4(4, u, k)
        E3V = lambda u, k: zq4(2, u, k)
        E3S = lambda u, k: zq4(3, u, k)

        def rawsl(d, u):
            return d.ap()[(u // 2) * 128:(u // 2 + 1) * 128,
                          (u % 2) * 1024:(u % 2) * 1024 + 1024]
        E4V = lambda c, k: zq4(3, c, k)
        E4S = lambda c, k: zq4(2, c, k)
        PROJ_BASE = {"q": 0, "k": NT, "v": NT + NTK, "o": NT + NTK + NTK}
        AX0 = (NTK + 1) // 2
        AX1 = NTK // 2

        block = ctx.enter_context(nc.Block())

        @block.sync
        def _(sy):
            for i in range(NTK):
                if i >= 2:
                    sy.wait_ge(a_v, (i - 2) * 5 + 4)
                    sy.wait_ge(a_s, (i - 2) * 2 + 1)
                sy.dma_start(out=xt[i % 2][:, :],
                             in_=x_sh.ap()[i * 128:(i + 1) * 128, :]
                             ).then_inc(s_x[i % 2], 16)
                if i == 11:
                    sy.wait_ge(g_xn, 16 * 8)
                    for j in range(NDC):
                        sy.dma_start(out=ab[:, XNc + j * TKV:
                                            XNc + j * TKV + 1024],
                                     in_=d_xn.ap()[0:1024, j * 128:(j + 1) * 128],
                                     transpose=True).then_inc(s_tr, 16)
            sy.wait_ge(g_xn, 16 * NTK)
            for j in range(NDC):
                sy.dma_start(out=ab[:, XNc + j * TKV + 1024:
                                    XNc + (j + 1) * TKV],
                             in_=d_xn.ap()[1024:TKV, j * 128:(j + 1) * 128],
                             transpose=True).then_inc(s_tr, 16)
            for h in range(NH):
                for s in range(3):
                    c0 = (h * 3 + s) * 128
                    sy.dma_start(out=E[:, c0:c0 + 128],
                                 in_=etab.ap()[c0:c0 + 128, :]).then_inc(s_e, 16)
            sy.wait_ge(g_qk, 16 * (NT + NTK))
            for j in range(NDC):
                sy.dma_start(out=ab[:, j * TOWN:(j + 1) * TOWN],
                             in_=d_q.ap()[:, j * 128:(j + 1) * 128],
                             transpose=True).then_inc(s_tr, 16)
            for j in range(NDC):
                sy.dma_start(out=ab[:, KTc + j * TKV: KTc + (j + 1) * TKV],
                             in_=d_k.ap()[:, j * 128:(j + 1) * 128],
                             transpose=True).then_inc(s_tr, 16)
            sy.wait_ge(g_o, 16 * NT)
            for j in range(NDC):
                sy.dma_start(out=ab[:, j * TOWN:(j + 1) * TOWN],
                             in_=d_o.ap()[:, j * 128:(j + 1) * 128],
                             transpose=True).then_inc(s_tr, 16)
            sy.wait_ge(c_w[0], 1)
            sy.wait_ge(c_w[1], 1)
            sy.wait_ge(p_mm, PROJ_BASE["o"] + NT)
            for c in range(NI):
                if c >= 2:
                    sy.wait_ge(e_mm, c - 1)
                base = (c % 2) * 4096
                sy.dma_start(out=wbuf[:, base:base + 1024].rearrange(
                                 "p (j i) -> p j i", i=128),
                             in_=ug_out.ap()[:, c * 128:(c + 1) * 128].rearrange(
                                 "(j p) i -> p j i", p=128)
                             ).then_inc(s_wg[c % 2], 16)
                sy.dma_start(out=wbuf[:, base + 1024:base + 2048].rearrange(
                                 "p (j i) -> p j i", i=128),
                             in_=gg_out.ap()[:, c * 128:(c + 1) * 128].rearrange(
                                 "(j p) i -> p j i", p=128)
                             ).then_inc(s_wg[c % 2], 16)
            sy.wait_ge(g_raw, 16 * 2 * NI)
            for c in range(NI):
                if c >= 2 and (c - 2) % 2 == 1:
                    j = c - 2
                    sy.wait_ge(g_gis, 16 * ((j - 1) // 2))
                    sy.wait_ge(e2v, E2V(2 * j + 1, 3))
                    sy.dma_start(out=d_gateraw.ap()[j * 128:(j + 1) * 128, :],
                                 in_=GA2p[j % 2]).then_inc(g_gis, 16)
                if c >= 1:
                    sy.wait_ge(s_rru, 16 * c)
                if c >= 2:
                    sy.wait_ge(e2v, E2V(2 * c - 3, 3))
                    sy.wait_ge(e2s, E2S(2 * c - 3, 3))
                sy.dma_start(out=UP2p[c % 2],
                             in_=d_upraw.ap()[c * 128:(c + 1) * 128, :]
                             ).then_inc(s_rru, 16)
            sy.wait_ge(g_gis, 16 * 11)
            sy.wait_ge(e2v, E2V(2 * 23 + 1, 3))
            sy.dma_start(out=d_gateraw.ap()[23 * 128:24 * 128, :],
                         in_=GA2p[1]).then_inc(g_gis, 16)

            sy.wait_ge(g_gi, 16 * 12)
            sy.wait_ge(g_gis, 16 * 12)
            sy.wait_ge(e2v, 5 * NI2)
            for c in range(NI):
                if c >= 1:
                    sy.wait_ge(s_gil, 16 * c)
                if c >= 2:
                    sy.wait_ge(e3v, E3V(2 * c - 3, 1))
                sy.dma_start(out=GR3p[c % 2],
                             in_=d_gateraw.ap()[c * 128:(c + 1) * 128, :]
                             ).then_inc(s_gil, 16)
            sy.wait_ge(g_g2, 16 * NI)
            sy.wait_ge(e3v, 2 * NI2)
            sy.wait_ge(e_mm, NI)
            for tg in range(4):
                for c in range(NI):
                    gi = tg * NI + c
                    if gi >= 1:
                        sy.wait_ge(s_wd[(gi - 1) % 2], 16 * ((gi - 1) // 2 + 1))
                    if gi >= 8:
                        sy.wait_ge(d_mm, gi - 7)
                    base = (gi % 8) * 1024
                    sy.dma_start(out=wbuf[:, base:base + 1024],
                                 in_=dg_out.ap()[c * 128:(c + 1) * 128, :]
                                 ).then_inc(s_wd[gi % 2], 16)

                for tt in range(4):
                    t = tg * 4 + tt
                    par = tt % 2
                    if t >= 1:
                        sy.wait_ge(f_ev, (t - 1) * 4 + 4)
                        sy.dma_start(out=out_sh.ap()[(t - 1) * 128:t * 128, :],
                                     in_=f32a[:, 1024:2048]).then_inc(g_st, 16)
                    nx0 = (AX0 if par == 0 else AX1) + tg * 2 + tt // 2
                    sy.wait_ge(s_x[par], 16 * nx0)
                    if t >= 2:
                        sy.wait_ge(f_ev, (t - 2) * 4 + 4)
                    sy.dma_start(out=xt[par][:, :],
                                 in_=x_sh.ap()[t * 128:(t + 1) * 128, :]
                                 ).then_inc(s_x[par], 16)
                    if t >= 1:
                        sy.wait_ge(s_att, 16 * t)
                    sy.dma_start(out=f32b[:, :],
                                 in_=d_att.ap()[t * 128:(t + 1) * 128, :]
                                 ).then_inc(s_att, 16)
            sy.wait_ge(f_ev, 15 * 4 + 4)
            sy.dma_start(out=out_sh.ap()[15 * 128:16 * 128, :],
                         in_=f32a[:, 1024:2048]).then_inc(g_st, 16)

        @block.gpsimd
        def _(g):
            g.dma_start(out=idn[:, :], in_=ident_d.ap()).then_inc(g_idl, 16)
            g.dma_start(out=af[:, 0:3072].rearrange("p (t d) -> p t d", d=128),
                        in_=up_sl.ap().rearrange("(t p) d -> p t d", p=128)
                        ).then_inc(g_wsl[0], 16)
            g.dma_start(out=af[:, 3072:6144].rearrange("p (t d) -> p t d", d=128),
                        in_=gate_sl.ap().rearrange("(t p) d -> p t d", p=128)
                        ).then_inc(g_wsl[0], 16)
            g.dma_start(out=af[:, 6144:9216].rearrange("p (t d) -> p t d", d=384),
                        in_=down_sl.ap().rearrange("(t p) d -> p t d", p=128)
                        ).then_inc(g_wsl[0], 16)
            def xn_store(i):
                g.wait_ge(a_v, i * 5 + 4)
                g.dma_start(out=d_xn.ap()[i * 128:(i + 1) * 128, :],
                            in_=f32a[:, (i % 2) * 1024:(i % 2) * 1024 + 1024]
                            ).then_inc(g_xn, 16)

            for gi, widx, ti, nb, ncols, nsub, sb in WMETA:
                for s in range(nsub):
                    S = sb + s
                    g.wait_ge(w_e, S + 1)
                    if widx < 2:
                        dst = W_AGIN[widx].ap()[:, ti * 128:(ti + 1) * 128]
                    else:
                        dst = W_AGIN[2].ap()[s * 128:(s + 1) * 128,
                                             ti * 128:(ti + 1) * 128]
                    g.dma_start(out=dst, in_=tst[S % 2][:, :]
                                ).then_inc(g_wag[S % 2], 16)
                    if (S + 1) % 4 == 0 and (S + 1) // 4 - 1 < 12:
                        xn_store((S + 1) // 4 - 1)
                    if (S + 1) % 4 == 0 and (S + 1) // 4 - 1 < NDC:
                        jq = (S + 1) // 4 - 1
                        g.dma_start(out=wbuf[:, jq * 1024:(jq + 1) * 1024],
                                    in_=wT["q"].ap()[jq * 128:(jq + 1) * 128, :]
                                    ).then_inc(s_wq, 16)
            for i in range(12, NTK):
                xn_store(i)

            def ar_stage(k):
                mb, gb = 4 * k, 64 * k
                g.wait_ge(mrv, mb + 1)
                g.dma_start(out=d_pr[k].ap(), in_=red2[:, :]).then_inc(g_pr, 16)
                g.wait_ge(g_pr, gb + 16)
                g.dma_start(out=rowb[0:1, :],
                            in_=d_pr[k].ap().rearrange("p c -> (p c)").unsqueeze(0)
                            ).then_inc(g_pr, 16)
                g.wait_ge(mrv, mb + 2)
                g.dma_start(out=cc_in[k].ap(), in_=locm[:, :]
                            ).then_inc(g_pr, 16)
                g.wait_ge(g_pr, gb + 48)
                g.collective_compute("AllGather", ALU.bypass,
                                     replica_groups=[list(range(NCORE))],
                                     ins=[cc_in[k].ap().opt()],
                                     outs=[cc_out[k].ap().opt()]).then_inc(c_m[k])
                g.wait_ge(c_m[k], 1)
                g.dma_start(out=garow[0:1, :],
                            in_=cc_out[k].ap().rearrange("r c -> (r c)").unsqueeze(0)
                            ).then_inc(g_pr, 16)

            def attw_load(p):
                if p in ("k", "v", "o"):
                    g.wait_ge(p_mm, PROJ_BASE[p])
                for j in range(NDC):
                    g.dma_start(out=wbuf[:, j * 1024:(j + 1) * 1024],
                                in_=wT[p].ap()[j * 128:(j + 1) * 128, :]
                                ).then_inc(g_w, 16)

            ar_stage(0)
            g.wait_ge(g_wag[0], 16 * ((NWSUB + 1) // 2))
            g.wait_ge(g_wag[1], 16 * (NWSUB // 2))
            for ccs, cin, cout in ((c_w[0], ug_in, ug_out),
                                   (c_w[1], gg_in, gg_out),
                                   (c_w[2], dg_in, dg_out)):
                g.collective_compute("AllGather", ALU.bypass,
                                     replica_groups=[list(range(NCORE))],
                                     ins=[cin.ap().opt()], outs=[cout.ap().opt()]
                                     ).then_inc(ccs)
            for idx in range(NT):
                g.wait_ge(l2_v, (idx + 1) * 2)
                g.dma_start(out=d_q.ap()[idx * 128:(idx + 1) * 128, :],
                            in_=O_sb[:, :]).then_inc(g_qk, 16)
            attw_load("k")
            for ti in range(NTK):
                g.wait_ge(l2_v, (NT + ti + 1) * 2)
                g.dma_start(out=d_k.ap()[ti * 128:(ti + 1) * 128, :],
                            in_=O_sb[:, :]).then_inc(g_qk, 16)
            attw_load("v")
            attw_load("o")
            for T in range(NT):
                g.wait_ge(b_on, T + 1)
                g.dma_start(out=d_o.ap()[T * 128:(T + 1) * 128, :],
                            in_=O_sb[:, :]).then_inc(g_o, 16)
            for tc in range(NT):
                g.wait_ge(p_ev, PROJ_BASE["o"] + tc + 1)
                g.dma_start(out=d_att.ap()[tc * 128:(tc + 1) * 128, :],
                            in_=f32a[:, (tc % 2) * 1024:(tc % 2) * 1024 + 1024]
                            ).then_inc(g_attw, 16)
            for c in range(NI):
                g.wait_ge(e_v, c + 1)
                g.dma_start(out=d_upraw.ap()[c * 128:(c + 1) * 128, :],
                            in_=upst[c % 2]).then_inc(g_raw, 16)
                g.dma_start(out=d_gateraw.ap()[c * 128:(c + 1) * 128, :],
                            in_=gast[c % 2]).then_inc(g_raw, 16)
            ar_stage(1)
            g.wait_ge(g_raw, 16 * 2 * NI)
            for c in range(NI):
                if c >= 2 and (c - 2) % 2 == 0:
                    j = c - 2
                    g.wait_ge(g_gi, 16 * (j // 2))
                    g.wait_ge(e2v, E2V(2 * j + 1, 3))
                    g.dma_start(out=d_gateraw.ap()[j * 128:(j + 1) * 128, :],
                                in_=GA2p[j % 2]).then_inc(g_gi, 16)
                if c >= 1:
                    g.wait_ge(s_rrg, 16 * c)
                if c >= 2:
                    g.wait_ge(e2v, E2V(2 * c - 3, 4))
                    if c % 2 == 0:
                        g.wait_ge(g_gi, 16 * (c // 2))
                    else:
                        g.wait_ge(g_gis, 16 * ((c - 1) // 2))
                g.dma_start(out=GA2p[c % 2],
                            in_=d_gateraw.ap()[c * 128:(c + 1) * 128, :]
                            ).then_inc(s_rrg, 16)
            g.wait_ge(g_gi, 16 * 11)
            g.wait_ge(e2v, E2V(2 * 22 + 1, 3))
            g.dma_start(out=d_gateraw.ap()[22 * 128:23 * 128, :],
                        in_=GA2p[0]).then_inc(g_gi, 16)
            ar_stage(2)
            for c in range(NI):
                if c >= 1:
                    g.wait_ge(g_g2, 16 * c)
                g.wait_ge(e3s, E3S(2 * c + 1, 1))
                g.dma_start(out=d_g2int.ap()[c * 128:(c + 1) * 128, :],
                            in_=I163p[c % 2]).then_inc(g_g2, 16)
            ar_stage(3)
            g.wait_ge(g_g2, 16 * NI)
            g.wait_ge(e3v, 2 * NI2)
            for tg4 in range(4):
                for c4 in range(NI):
                    gi4 = tg4 * NI + c4
                    if gi4 >= 4:
                        g.wait_ge(e4v, E4V(gi4 - 4, 2))
                    g.dma_start(out=i16s[:, (gi4 % 4) * 512:(gi4 % 4) * 512 + 512],
                                in_=d_g2int.ap()[c4 * 128:(c4 + 1) * 128,
                                                 tg4 * 512:(tg4 + 1) * 512]
                                ).then_inc(s_gl[gi4 % 4], 16)
            g.wait_ge(g_st, 16 * NT)
            if debug:
                nst = 0
                for o, src in ((dbg["sc"], sc_row[:, :]), ):
                    g.dma_start(out=o.ap(), in_=src).then_inc(g_st, 16)
                    nst += 1
                for o, dsrc in ((dbg["att"], d_att), (dbg["xn"], d_xn),
                                (dbg["q"], d_q), (dbg["k"], d_k),
                                (dbg["ug"], ug_out), (dbg["dg"], dg_out)):
                    g.dma_start(out=o.ap(), in_=dsrc.ap()).then_inc(g_st, 16)
                    nst += 1
                g.dma_start(out=dbg["raw"].ap()[0:128, :],
                            in_=d_upraw.ap()[0:128, :]).then_inc(g_st, 16)
                g.dma_start(out=dbg["raw"].ap()[128:256, :],
                            in_=d_gateraw.ap()[0:128, :]).then_inc(g_st, 16)
                nst += 2
                g.wait_ge(g_st, 16 * NT + 16 * nst)

        @block.vector
        def _(v):
            v.memset(ones_f[:, :], 1.0)
            v.memset(epsb[:, :], 1e-5)
            v.memset(magp[:, :], MAGIC)
            v.memset(magn[:, :], -MAGIC)
            v.memset(red2[:, :], 0.0)
            v.memset(locm[:, :], 0.0)
            v.memset(sc_row[:, :], 0.0)
            WV6 = lambda c, k: zpos(6, c, k)

            def w_op(gi, k):
                _, widx, ti, nb, ncols, nsub, sb = WMETA[gi]
                p = gi % 2
                buf = af[:, widx * 3072 + ti * ncols:
                         widx * 3072 + (ti + 1) * ncols]
                wq, wf, we = wqs2[p], wfl2[p], weff2[p]
                if k == 0:
                    v.wait_ge(g_wsl[0], 16 * (widx + 1))
                    if gi >= 2:
                        v.wait_ge(w_v, WV6(gi - 2, 5))
                    v.tensor_reduce(wq[:, 0:nb],
                                    buf[:, 0:ncols].rearrange("p (b k) -> p b k",
                                                              k=32),
                                    AX.X, ALU.max,
                                    apply_absolute_value=True).then_inc(w_v, 1)
                elif k == 1:
                    v.wait_ge(w_v, WV6(gi, 0))
                    v.tensor_scalar(wq[:, 16:16 + nb], wq[:, 0:nb], INV_QW, 1e-12,
                                    ALU.mult, ALU.max).then_inc(w_v, 1)
                elif k == 2:
                    v.wait_ge(w_v, WV6(gi, 1))
                    v.reciprocal(wq[:, 32:32 + nb], wq[:, 16:16 + nb]
                                 ).then_inc(w_v, 1)
                elif k == 3:
                    v.wait_ge(w_v, WV6(gi, 2))
                    v.tensor_tensor(wf[:, 0:ncols].rearrange("p (b k) -> p b k",
                                                             k=32),
                                    buf[:, 0:ncols].rearrange("p (b k) -> p b k",
                                                              k=32),
                                    wq[:, 32:32 + nb].unsqueeze(2).broadcast_to(
                                        (128, nb, 32)),
                                    ALU.mult).then_inc(w_v, 1)
                elif k == 4:
                    v.wait_ge(w_v, WV6(gi, 3))
                    v.tensor_scalar(wf[:, 0:ncols], wf[:, 0:ncols], MAGIC, MAGIC,
                                    ALU.add, ALU.subtract).then_inc(w_v, 1)
                elif k == 5:
                    v.wait_ge(w_v, WV6(gi, 4))
                    if gi >= 2:
                        v.wait_ge(w_t, WMETA[gi - 2][6] + WMETA[gi - 2][5])
                    v.tensor_tensor(we[:, 0:ncols].rearrange("p (b k) -> p b k",
                                                             k=32),
                                    wf[:, 0:ncols].rearrange("p (b k) -> p b k",
                                                             k=32),
                                    wq[:, 16:16 + nb].unsqueeze(2).broadcast_to(
                                        (128, nb, 32)),
                                    ALU.mult).then_inc(w_v, 1)
                else:
                    for s in range(nsub):
                        S = sb + s
                        v.wait_ge(w_t, S + 1)
                        if S >= 2:
                            v.wait_ge(g_wag[S % 2], 16 * (S // 2))
                        psrc = (psp if S % 2 == 0 else pp[0])[:, 0:64].bitcast(BF16)
                        v.tensor_copy(tst[S % 2][:, :], psrc).then_inc(w_e, 1)

            def w_pair(cp):
                for k in range(7):
                    w_op(cp, k)
                    if cp + 1 < NWTL:
                        w_op(cp + 1, k)

            def a_tile(i):
                xb = xt[i % 2]
                v.wait_ge(s_x[i % 2], 16 * (i // 2 + 1))
                v.reduce_sum(stats[:, 0:1], xb[:, :], AX.X).then_inc(a_v, 1)
                v.wait_ge(a_s, i * 2 + 1)
                v.wait_ge(a_v, i * 5 + 1)
                vst(v.tensor_scalar(stats[:, 2:3], stats[:, 0:1], 1.0 / D, None,
                                    ALU.mult))
                v.tensor_scalar(stats[:, 3:4], stats[:, 1:2], 1.0 / D, None, ALU.mult)
                v.wait_ge(vs, vsn[0])
                vst(v.tensor_tensor(stats[:, 4:5], stats[:, 2:3], stats[:, 2:3],
                                    ALU.mult))
                v.wait_ge(vs, vsn[0])
                v.tensor_tensor(stats[:, 5:6], stats[:, 3:4], stats[:, 4:5],
                                ALU.subtract).then_inc(a_v, 1)
                v.wait_ge(a_s, i * 2 + 2)
                v.scalar_tensor_tensor(stats[:, 8:9], stats[:, 2:3], -1.0,
                                       stats[:, 7:8], ALU.mult, ALU.mult
                                       ).then_inc(a_v, 1)
                half = (i % 2) * 1024
                if i >= 2:
                    v.wait_ge(g_xn, 16 * (i - 1))
                v.wait_ge(a_v, i * 5 + 3)
                v.scalar_tensor_tensor(
                    f32a[:, half:half + 1024], xb[:, :], stats[:, 7:8],
                    stats[:, 8:9].broadcast_to((128, 1024)),
                    ALU.mult, ALU.add).then_inc(a_v, 1)
                v.wait_ge(a_v, i * 5 + 4)
                v.tensor_reduce(m0acc[:, i:i + 1], f32a[:, half:half + 1024],
                                AX.X, ALU.max, apply_absolute_value=True
                                ).then_inc(a_v, 1)

            for st2 in range(28):
                w_pair(2 * st2)
                if st2 % 2 == 0 and st2 // 2 < NTK:
                    a_tile(st2 // 2)
            for i in range(14, NTK):
                a_tile(i)

            def ar_vec(k, slots, accs, waits):
                mb, gb = 4 * k, 64 * k
                for sm, val in waits:
                    v.wait_ge(sm, val)
                for j, acc in enumerate(accs):
                    ins = v.tensor_reduce(red2[:, j:j + 1], acc, AX.X, ALU.max)
                    if j == len(accs) - 1:
                        ins.then_inc(mrv, 1)
                v.wait_ge(g_pr, gb + 32)
                v.tensor_reduce(locm[0:1, 0:2],
                                rowb[0:1, 0:256].rearrange("p (q c) -> p c q", c=2),
                                AX.X, ALU.max).then_inc(mrv, 1)
                v.wait_ge(g_pr, gb + 64)
                vst(v.tensor_reduce(locm[0:1, 0:2],
                                    garow[0:1, 0:64].rearrange("p (r c) -> p c r",
                                                               c=8)[:, 0:2, :],
                                    AX.X, ALU.max))
                last = None
                for j, (ssl, rsl) in enumerate(slots):
                    v.wait_ge(vs, vsn[0])
                    vst(v.tensor_scalar(sc_row[0:1, ssl:ssl + 1], locm[0:1, j:j + 1],
                                        INV_QA, 1e-12, ALU.mult, ALU.max))
                    v.wait_ge(vs, vsn[0])
                    last = v.reciprocal(sc_row[0:1, rsl:rsl + 1],
                                        sc_row[0:1, ssl:ssl + 1])
                    if k == 1 or (j, (ssl, rsl)) != (len(slots) - 1, slots[-1]):
                        vst(last)
                if k == 1:
                    v.wait_ge(vs, vsn[0])
                    last = v.tensor_scalar(sc_row[0:1, SGQ:SGQ + 1],
                                           sc_row[0:1, SG:SG + 1],
                                           1.0 / 65536.0, None, ALU.mult)
                last.then_inc(mrv, 1)
                v.wait_ge(scb, k + 1)
                v.tensor_copy(SC[:, :], psp[:, 0:16]).then_inc(mrv, 1)

            ar_vec(0, [(S0, R0)], [m0acc[:, :]], [(a_v, 5 * NTK)])

            for proj in ("q", "k", "v"):
                ntl = NT if proj == "q" else NTK
                for tc in range(ntl):
                    idx = PROJ_BASE[proj] + tc
                    par = tc % 2
                    bk = [pp[2 * par], pp[2 * par + 1]]
                    v.wait_ge(p_mm, idx + 1)
                    if proj in ("q", "k"):
                        qki = idx if proj == "q" else NT + tc
                        v.tensor_copy(f32a[:, 0:512], bk[0][:, :])
                        v.tensor_copy(f32a[:, 512:1024], bk[1][:, :]
                                      ).then_inc(p_ev, 1)
                        v.wait_ge(l2_s, qki * 2 + 1)
                        vst(v.tensor_reduce(ss[:, 0:16],
                                            f32b[:, :].rearrange("p (h e) -> p h e",
                                                                 e=64),
                                            AX.X, ALU.add))
                        v.wait_ge(vs, vsn[0])
                        v.tensor_scalar(ss[:, 0:16], ss[:, 0:16], 1e-12, None,
                                        ALU.max).then_inc(l2_v, 1)
                        v.wait_ge(l2_s, qki * 2 + 2)
                        v.wait_ge(l2_v, qki * 2 + 1)
                        vst(v.tensor_scalar(ss[:, 0:16], ss[:, 0:16], 1e6, None,
                                            ALU.min))
                        if qki >= 1:
                            v.wait_ge(g_qk, 16 * qki)
                        v.wait_ge(vs, vsn[0])
                        v.tensor_tensor(
                            O_sb[:, :].rearrange("p (h e) -> p h e", e=64),
                            f32a[:, 0:1024].rearrange("p (h e) -> p h e", e=64),
                            ss[:, 0:16].unsqueeze(2).broadcast_to((128, 16, 64)),
                            ALU.mult).then_inc(l2_v, 1)
                    else:
                        vt = tc
                        base = vt * NH * 65
                        v.tensor_copy(
                            V[:, base:base + 520].rearrange("p (h e) -> p h e",
                                                            e=65)[:, :, 0:64],
                            bk[0][:, :].rearrange("p (h e) -> p h e", e=64))
                        v.tensor_copy(
                            V[:, base + 520:base + 1040].rearrange(
                                "p (h e) -> p h e", e=65)[:, :, 0:64],
                            bk[1][:, :].rearrange("p (h e) -> p h e", e=64))
                        v.memset(V[:, base:base + 1040].rearrange(
                            "p (h e) -> p h e", e=65)[:, :, 64:65], 1.0
                            ).then_inc(p_ev, 1)

            v.wait_ge(s_e, 16 * NH * 3)
            for T in range(NT):
                for h in range(NH):
                    gx = T * NH + h
                    par = gx % 4
                    v.wait_ge(b_exp, gx + 1)
                    if gx >= 4:
                        v.wait_ge(b_av, gx - 3)
                    if T < NT - 1:
                        esl = E[:, h * 384:h * 384 + 256].rearrange(
                            "p (c q) -> p c q", q=128)
                    else:
                        esl = E[:, h * 384:h * 384 + 384].rearrange(
                            "p (c q) -> p c q", q=128)[:, 0:3:2, :]
                    v.tensor_tensor(PT[par][:, :].rearrange("p (c q) -> p c q", q=128),
                                    f32b[:, par * 256:(par + 1) * 256].rearrange(
                                        "p (c q) -> p c q", q=128),
                                    esl, ALU.mult).then_inc(b_pt, 1)
                v.wait_ge(b_av, (T + 1) * NH)
                if T == 0:
                    v.wait_ge(g_qk, 16 * (NT + NTK))
                if T >= 1:
                    v.wait_ge(g_o, 16 * T)
                v.reciprocal(
                    recs[:, 0:14].rearrange("p (b j) -> p b j", j=7),
                    pbO[:, 0:1024].rearrange("p (b q) -> p b q", q=512)[:, :, 64:512:65])
                vst(v.reciprocal(recs[:, 14:16], pbO[:, 1088:1154:65]))
                v.wait_ge(vs, vsn[0])
                for b in range(3):
                    nhh = 7 if b < 2 else 2
                    v.tensor_tensor(
                        O_sb[:, b * 448:b * 448 + nhh * 64].rearrange(
                            "p (h e) -> p h e", e=64),
                        pbO[:, b * 512:b * 512 + nhh * 65].rearrange(
                            "p (h e) -> p h e", e=65)[:, :, 0:64],
                        recs[:, b * 7:b * 7 + nhh].unsqueeze(2).broadcast_to(
                            (128, nhh, 64)),
                        ALU.mult).then_inc(b_on, 1 if b == 2 else 0)

            for j in range(NDC):
                v.wait_ge(xq_s, j + 1)
                v.tensor_scalar(ab[:, j * TOWN:(j + 1) * TOWN], Fal2,
                                MAGIC, SC[:, S0:S0 + 1], ALU.subtract,
                                ALU.mult).then_inc(xq_v, 1)
            for c in range(NI):
                v.wait_ge(e_ev, 8 * (c + 1))
                v.tensor_reduce(mupacc[:, c:c + 1], upst[c % 2], AX.X, ALU.max,
                                apply_absolute_value=True)
                v.tensor_reduce(mgacc[:, c:c + 1], gast[c % 2], AX.X, ALU.max,
                                apply_absolute_value=True).then_inc(e_v, 1)
            ar_vec(1, [(SUP, RUP), (SG, RG)], [mupacc[:, :], mgacc[:, :]],
                   [(e_v, NI)])
            def e2v_op(u, k):
                p = u % 4
                if k == 0:
                    v.wait_ge(e2s, E2S(u, 2))
                    v.tensor_scalar(UP2[p], T12[p], MAGIC, SC[:, SGQ:SGQ + 1],
                                    ALU.subtract, ALU.mult).then_inc(e2v, 1)
                elif k == 1:
                    v.wait_ge(e2s, E2S(u, 3))
                    v.wait_ge(e2v, E2V(u, 0))
                    v.tensor_scalar(T12[p], GA2[p], 65536.0, 65535.0, ALU.mult,
                                    ALU.min).then_inc(e2v, 1)
                elif k == 2:
                    v.wait_ge(e2v, E2V(u, 1))
                    v.tensor_scalar(T12[p], T12[p], MAGIC, None,
                                    ALU.add).then_inc(e2v, 1)
                elif k == 3:
                    v.wait_ge(e2v, E2V(u, 2))
                    v.scalar_tensor_tensor(GA2[p], T12[p], MAGIC, UP2[p],
                                           ALU.subtract, ALU.mult
                                           ).then_inc(e2v, 1)
                else:
                    v.wait_ge(e2v, E2V(u, 3))
                    v.tensor_reduce(mg2acc[:, u:u + 1], GA2[p], AX.X, ALU.max,
                                    apply_absolute_value=True).then_inc(e2v, 1)

            for q0 in range(0, NI2, 4):
                for k in range(5):
                    for u in range(q0, q0 + 4):
                        e2v_op(u, k)
            ar_vec(2, [(SG2, RG2)], [mg2acc[:, :]], [(e2v, 5 * NI2)])
            def e3v_op(u, k):
                p = u % 4
                if k == 0:
                    v.wait_ge(e3s, E3S(u, 2))
                    v.scalar_tensor_tensor(GR3[p], I163[p], SC[:, SG2:SG2 + 1],
                                           UQ3[p], ALU.mult, ALU.mult
                                           ).then_inc(e3v, 1)
                else:
                    v.wait_ge(e3v, E3V(u, 0))
                    v.tensor_reduce(moacc[:, u:u + 1], GR3[p], AX.X, ALU.max,
                                    apply_absolute_value=True).then_inc(e3v, 1)

            for q0 in range(0, NI2, 4):
                for k in range(2):
                    for u in range(q0, q0 + 4):
                        e3v_op(u, k)
            ar_vec(3, [(SO, RO)], [moacc[:, :]], [(e3v, 2 * NI2)])
            def e4v_op(tg, c, k):
                gi = tg * NI + c
                q = QA4[gi % 4]
                if k == 0:
                    v.wait_ge(e4s, E4S(gi, 1))
                    if gi >= 4:
                        v.wait_ge(e4v, E4V(gi - 4, 2))
                    v.tensor_tensor(q[:, 1024:1536], q[:, 0:512],
                                    q[:, 512:1024], ALU.mult).then_inc(e4v, 1)
                elif k == 1:
                    v.wait_ge(e4v, E4V(gi, 0))
                    v.tensor_scalar(q[:, 1536:2048], q[:, 1024:1536],
                                    SC[:, RO:RO + 1], MAGIC, ALU.mult, ALU.add
                                    ).then_inc(e4v, 1)
                else:
                    v.wait_ge(e4v, E4V(gi, 1))
                    if gi >= 4:
                        v.wait_ge(d_mm, gi - 3)
                    v.tensor_scalar(ot[gi % 4][:, :], q[:, 1536:2048],
                                    MAGIC, SC[:, SO:SO + 1],
                                    ALU.subtract, ALU.mult).then_inc(e4v, 1)

            for tg in range(4):
                for q0 in range(0, NI, 4):
                    for k in range(3):
                        for cc2 in range(q0, q0 + 4):
                            e4v_op(tg, cc2, k)
                for tt in range(4):
                    t = tg * 4 + tt
                    par = tt % 2
                    v.wait_ge(f_ev, 4 * t + 2)
                    nx = (AX0 if par == 0 else AX1) + tg * 2 + tt // 2 + 1
                    v.wait_ge(s_x[par], 16 * nx)
                    v.wait_ge(s_att, 16 * (t + 1))
                    if t >= 1:
                        v.wait_ge(g_st, 16 * t)
                    v.tensor_tensor(f32a[:, 1024:2048], xt[par][:, :], f32b[:, :],
                                    ALU.add).then_inc(f_ev, 1)
                    v.wait_ge(f_ev, 4 * t + 3)
                    v.tensor_tensor(f32a[:, 1024:2048], f32a[:, 1024:2048],
                                    f32a[:, 0:1024], ALU.add).then_inc(f_ev, 1)

        @block.scalar
        def _(sc):
            for i in range(NTK):
                xb = xt[i % 2]
                sc.wait_ge(a_v, 5 * i + 1)
                sc.activation(f32b[:, :], xb[:, :], AF.Square,
                              accum_out=stats[:, 1:2]).then_inc(a_s, 1)
                sc.wait_ge(a_v, 5 * i + 2)
                sst(sc.activation(stats[:, 6:7], stats[:, 5:6], AF.Ln,
                                  bias=epsb[:, 0:1]))
                sc.wait_ge(ssf, ssn[0])
                sc.activation(stats[:, 7:8], stats[:, 6:7], AF.Exp,
                              scale=-0.5).then_inc(a_s, 1)
            for proj in ("q", "k"):
                ntl = NT if proj == "q" else NTK
                for tc in range(ntl):
                    qki = tc if proj == "q" else NT + tc
                    idx = PROJ_BASE[proj] + tc
                    sc.wait_ge(p_ev, idx + 1)
                    sc.activation(f32b[:, :], f32a[:, 0:1024], AF.Square
                                  ).then_inc(l2_s, 1)
                    sc.wait_ge(l2_v, qki * 2 + 1)
                    sst(sc.activation(ss[:, 16:32], ss[:, 0:16], AF.Ln))
                    sc.wait_ge(ssf, ssn[0])
                    sc.activation(ss[:, 0:16], ss[:, 16:32], AF.Exp,
                                  scale=-0.5).then_inc(l2_s, 1)
            for T in range(NT):
                for h in range(NH):
                    gx = T * NH + h
                    par = gx % 4
                    sc.wait_ge(b_mm, gx + 1)
                    if gx >= 3:
                        sc.wait_ge(b_pt, gx - 2)
                    sc.activation(f32b[:, par * 256:(par + 1) * 256],
                                  pp[par][:, 0:256], AF.Exp).then_inc(b_exp, 1)
            for tc in range(NT):
                idx = PROJ_BASE["o"] + tc
                par = tc % 2
                sc.wait_ge(p_mm, idx + 1)
                if tc >= 2:
                    sc.wait_ge(g_attw, 16 * (tc - 1))
                sc.copy(f32a[:, par * 1024:par * 1024 + 512], pp[2 * par][:, :])
                sc.copy(f32a[:, par * 1024 + 512:par * 1024 + 1024],
                        pp[2 * par + 1][:, :]).then_inc(p_ev, 1)
            sc.wait_ge(p_mm, PROJ_BASE["o"] + NT)
            sc.wait_ge(b_pt, NT * NH)
            sc.wait_ge(mrv, 4)
            for j in range(NDC):
                if j >= 1:
                    sc.wait_ge(xq_v, j)
                sc.activation(Fal2, ab[:, XNc + j * TKV: XNc + j * TKV + 2048],
                              AF.Copy, bias=MAGIC,
                              scale=SC[:, R0:R0 + 1]).then_inc(xq_s, 1)
            sc.wait_ge(g_attw, 16 * NT)
            sc.wait_ge(b_pt, NT * NH)
            for c in range(NI):
                sc.wait_ge(e_mm, c + 1)
                if c >= 2:
                    sc.wait_ge(g_raw, 32 * (c - 1))
                    sc.wait_ge(e_v, c - 1)
                ub, gb2 = upst[c % 2], gast[c % 2]
                for tsub in range(4):
                    sc.copy(ub[:, tsub * 512:(tsub + 1) * 512], pp[tsub][:, :]
                            ).then_inc(e_ev, 1)
                for tsub in range(3):
                    sc.copy(gb2[:, tsub * 512:(tsub + 1) * 512],
                            pbO[:, tsub * 512:(tsub + 1) * 512]).then_inc(e_ev, 1)
                sc.copy(gb2[:, 1536:2048], psp[:, :]).then_inc(e_ev, 1)
            sc.wait_ge(mrv, 8)

            def e2s_op(u, k):
                p = u % 4
                if k == 0:
                    sc.wait_ge(s_rru, 16 * (u // 2 + 1))
                    if u >= 4:
                        sc.wait_ge(e2s, E2S(u - 4, 1))
                    sc.activation(T12a[p], UP2[p], AF.Copy, bias=MAGIC,
                                  scale=SC[:, RUP:RUP + 1]).then_inc(e2s, 1)
                elif k == 1:
                    if u == 0:
                        sc.wait_ge(e_mm, NI)
                    sc.wait_ge(e2s, E2S(u, 0))
                    sc.activation(arena[:, u * 1024:(u + 1) * 1024], T12a[p],
                                  AF.Copy, bias=-MAGIC).then_inc(e2s, 1)
                elif k == 2:
                    sc.wait_ge(s_rrg, 16 * (u // 2 + 1))
                    if u >= 4:
                        sc.wait_ge(e2v, E2V(u - 4, 3))
                    sc.activation(T12[p], GA2[p], AF.Copy, bias=MAGIC,
                                  scale=SC[:, RG:RG + 1]).then_inc(e2s, 1)
                else:
                    sc.wait_ge(e2v, E2V(u, 0))
                    sc.wait_ge(e2s, E2S(u, 2))
                    sc.activation(GA2[p], UP2[p], AF.Sigmoid,
                                  scale=65536.0).then_inc(e2s, 1)

            for q0 in range(0, NI2, 4):
                for k in range(4):
                    for u in range(q0, q0 + 4):
                        e2s_op(u, k)

            sc.wait_ge(mrv, 12)

            def e3s_op(u, k):
                p = u % 4
                if k == 0:
                    sc.wait_ge(s_gil, 16 * (u // 2 + 1))
                    if u >= 4:
                        sc.wait_ge(e3s, E3S(u - 4, 1))
                    sc.activation(T13[p], GR3[p], AF.Copy, bias=MAGIC,
                                  scale=SC[:, RG2:RG2 + 1]).then_inc(e3s, 1)
                elif k == 1:
                    sc.wait_ge(e3s, E3S(u, 0))
                    if u >= 4:
                        sc.wait_ge(g_g2, 16 * (u // 2 - 1))
                        sc.wait_ge(e3v, E3V(u - 4, 0))
                    sc.activation(I163[p], T13[p], AF.Copy,
                                  bias=-MAGIC).then_inc(e3s, 1)
                else:
                    sc.wait_ge(e3s, E3S(u, 0))
                    if u >= 4:
                        sc.wait_ge(e3v, E3V(u - 4, 0))
                    sc.activation(UQ3[p], arena[:, u * 1024:(u + 1) * 1024],
                                  AF.Copy, scale=SC[:, SUP:SUP + 1]
                                  ).then_inc(e3s, 1)

            for q0 in range(0, NI2, 4):
                for k in range(3):
                    for u in range(q0, q0 + 4):
                        e3s_op(u, k)

            sc.wait_ge(mrv, 16)

            def e4s_op(tg, c, k):
                gi = tg * NI + c
                q = QA4[gi % 4]
                if k == 0:
                    sc.wait_ge(s_gl[gi % 4], 16 * (gi // 4 + 1))
                    if gi >= 4:
                        sc.wait_ge(e4v, E4V(gi - 4, 0))
                    sc.activation(q[:, 0:512],
                                  i16s[:, (gi % 4) * 512:(gi % 4) * 512 + 512],
                                  AF.Copy, scale=SC[:, SG2:SG2 + 1]).then_inc(e4s, 1)
                else:
                    sc.activation(q[:, 512:1024],
                                  arena[:, c * 2048 + tg * 512:
                                        c * 2048 + (tg + 1) * 512],
                                  AF.Copy, scale=SC[:, SUP:SUP + 1]).then_inc(e4s, 1)

            for tg in range(4):
                for q0 in range(0, NI, 4):
                    for k in range(2):
                        for cc2 in range(q0, q0 + 4):
                            e4s_op(tg, cc2, k)
                for tt in range(4):
                    t = tg * 4 + tt
                    sc.wait_ge(d_mm, NI * (tg + 1))
                    if t >= 1:
                        sc.wait_ge(f_ev, 4 * t)
                    sc.copy(f32a[:, 0:512], e4bank(tt, 0)).then_inc(f_ev, 1)
                    sc.copy(f32a[:, 512:1024], e4bank(tt, 1)).then_inc(f_ev, 1)

        @block.tensor
        def _(t):
            t.wait_ge(g_idl, 16)
            for gi, widx, ti, nb, ncols, nsub, sb in WMETA:
                t.wait_ge(w_v, zpos(6, gi, 5))
                we = weff2[gi % 2]
                for s in range(nsub):
                    S = sb + s
                    if S >= 2:
                        t.wait_ge(w_e, S - 1)
                    dst = (psp if S % 2 == 0 else pp[0])[:, 0:64].bitcast(BF16)
                    t.transpose(dst, we[:, s * 128:(s + 1) * 128],
                                idn[:, :]).then_inc(w_t, 1)
            t.wait_ge(w_e, NWSUB)
            t.wait_ge(mrv, 3)
            t.matmul(psp[:, 0:16], ones_f[0:1, 0:128], sc_row[0:1, 0:16],
                     start=True, stop=True).then_inc(scb, 1)
            for proj in ("q", "k", "v"):
                ntl = NT if proj == "q" else NTK
                if proj != "q":
                    t.wait_ge(g_w, GW_P[proj])
                    t.wait_ge(s_tr, 16 * 16)
                for tc in range(ntl):
                    idx = PROJ_BASE[proj] + tc
                    par = tc % 2
                    if idx >= 2:
                        t.wait_ge(p_ev, idx - 1)
                    if proj == "q" and tc == 8:
                        t.wait_ge(s_tr, 16 * 16)
                    for j in range(NDC):
                        if proj == "q" and tc == 0:
                            t.wait_ge(s_tr, 16 * (j + 1))
                            t.wait_ge(s_wq, 16 * (j + 1))
                        lhsT = ab[:, XNc + j * TKV + tc * 128:
                                  XNc + j * TKV + (tc + 1) * 128]
                        ins = t.matmul(pp[2 * par][:, :], lhsT,
                                       wbuf[:, j * 1024:j * 1024 + 512],
                                       start=(j == 0), stop=(j == NDC - 1))
                        ins2 = t.matmul(pp[2 * par + 1][:, :], lhsT,
                                        wbuf[:, j * 1024 + 512:(j + 1) * 1024],
                                        start=(j == 0), stop=(j == NDC - 1))
                        if j == NDC - 1:
                            ins2.then_inc(p_mm, 1)
            t.wait_ge(s_tr, 16 * 32)
            t.wait_ge(p_ev, PROJ_BASE["o"])

            def scores(T, h):
                gx = T * NH + h
                par = gx % 4
                if gx >= 4:
                    t.wait_ge(b_exp, gx - 3)
                j = h // 2
                r0 = (h % 2) * 64
                for c in range(2):
                    lhsT = ab[r0:r0 + 64,
                              KTc + j * TKV + T * 128 + c * 128:
                              KTc + j * TKV + T * 128 + (c + 1) * 128]
                    rhs = ab[r0:r0 + 64, j * TOWN + T * 128: j * TOWN + (T + 1) * 128]
                    ins = t.matmul(pp[par][:, c * 128:(c + 1) * 128], lhsT, rhs,
                                   start=True, stop=True)
                    if c == 1:
                        ins.then_inc(b_mm, 1)

            def av(T, h):
                gx = T * NH + h
                par = gx % 4
                t.wait_ge(b_pt, gx + 1)
                if h == 0:
                    t.wait_ge(b_on, T)
                ocol = (h // 7) * 512 + (h % 7) * 65
                for c in range(2):
                    vbase = (T + c) * NH * 65 + h * 65
                    ins = t.matmul(pbO[:, ocol:ocol + 65],
                                   PT[par][:, c * 128:(c + 1) * 128],
                                   V[:, vbase:vbase + 65],
                                   start=(c == 0), stop=(c == 1))
                    if c == 1:
                        ins.then_inc(b_av, 1)

            for T in range(NT):
                scores(T, 0)
                for h in range(1, NH):
                    scores(T, h)
                    av(T, h - 1)
                av(T, NH - 1)
            t.wait_ge(s_tr, 16 * 40)
            t.wait_ge(g_w, GW_P["o"])
            t.wait_ge(b_exp, NT * NH)
            for tc in range(NT):
                idx = PROJ_BASE["o"] + tc
                par = tc % 2
                if tc >= 2:
                    t.wait_ge(p_ev, idx - 1)
                for j in range(NDC):
                    lhsT = ab[:, j * TOWN + tc * 128: j * TOWN + (tc + 1) * 128]
                    ins = t.matmul(pp[2 * par][:, :], lhsT,
                                   wbuf[:, j * 1024:j * 1024 + 512],
                                   start=(j == 0), stop=(j == NDC - 1))
                    ins2 = t.matmul(pp[2 * par + 1][:, :], lhsT,
                                    wbuf[:, j * 1024 + 512:(j + 1) * 1024],
                                    start=(j == 0), stop=(j == NDC - 1))
                    if j == NDC - 1:
                        ins2.then_inc(p_mm, 1)
            t.wait_ge(xq_v, NDC)
            for c in range(NI):
                t.wait_ge(s_wg[c % 2], 32 * (c // 2 + 1))
                base = (c % 2) * 4096
                for j in range(NDC):
                    for tsub in range(4):
                        if j == 0 and c >= 1:
                            t.wait_ge(e_ev, 8 * (c - 1) + tsub + 1)
                        rhs = ab[:, j * TOWN + tsub * 512: j * TOWN + (tsub + 1) * 512]
                        t.matmul(pp[tsub][:, :], wbuf[:, base + j * 128:
                                                      base + (j + 1) * 128],
                                 rhs, start=(j == 0), stop=(j == NDC - 1))
                    for tsub in range(4):
                        if j == 0 and c >= 1:
                            t.wait_ge(e_ev, 8 * (c - 1) + 4 + tsub + 1)
                        rhs = ab[:, j * TOWN + tsub * 512: j * TOWN + (tsub + 1) * 512]
                        gb = (pbO[:, tsub * 512:(tsub + 1) * 512] if tsub < 3
                              else psp[:, :])
                        ins = t.matmul(gb, wbuf[:, base + 1024 + j * 128:
                                                base + 1024 + (j + 1) * 128],
                                       rhs, start=(j == 0), stop=(j == NDC - 1))
                        if j == NDC - 1 and tsub == 3:
                            ins.then_inc(e_mm, 1)
            for k in (1, 2, 3):
                t.wait_ge(mrv, 4 * k + 3)
                t.matmul(psp[:, 0:16], ones_f[0:1, 0:128], sc_row[0:1, 0:16],
                         start=True, stop=True).then_inc(scb, 1)
            for tg in range(4):
                for c in range(NI):
                    gi = tg * NI + c
                    t.wait_ge(s_wd[gi % 2], 16 * (gi // 2 + 1))
                    t.wait_ge(e4v, E4V(gi, 2))
                    if c == 0 and tg >= 1:
                        t.wait_ge(f_ev, 16 * tg)
                    base = (gi % 8) * 1024
                    for s in range(4):
                        for hf in range(2):
                            ins = t.matmul(e4bank(s, hf),
                                           ot[gi % 4][:, s * 128:(s + 1) * 128],
                                           wbuf[:, base + hf * 512:
                                                base + (hf + 1) * 512],
                                           start=(c == 0), stop=(c == NI - 1))
                            if s == 3 and hf == 1:
                                ins.then_inc(d_mm, 1)

    return nc


def _make_etab(dt, half):
    base = np.linspace(0.0, 1.0, NH, dtype=dt)
    slopes = np.power(np.asarray(2.0, dtype=dt), (-8.0 * base).astype(dt))
    k = np.arange(128)
    q = np.arange(128)
    E = np.zeros((NH, 3, 128, 128), dtype=dt)
    for h in range(NH):
        for slot in range(3):
            c = 1 if slot >= 1 else 0
            rel = (c * 128 + k[:, None]) - q[None, :]
            valid = (rel >= 0) & (rel < 128)
            if slot == 2 and half == 1:
                valid = np.zeros_like(valid)
            vals = np.exp((-rel * slopes[h]).astype(dt)).astype(dt)
            E[h, slot] = np.where(valid, vals, np.asarray(0.0, dtype=dt))
    return np.ascontiguousarray(E.reshape(NH * 3 * 128, 128))


def make_inputs(L0_attn_x, L0_attn_Wq, L0_attn_Wk, L0_attn_Wv, L0_attn_Wo,
                L0_ffn_w_gate, L0_ffn_w_up, L0_ffn_w_down):
    x = np.asarray(L0_attn_x)
    dt = x.dtype
    wq = np.ascontiguousarray(np.asarray(L0_attn_Wq).T)
    wk = np.ascontiguousarray(np.asarray(L0_attn_Wk).T)
    wv = np.ascontiguousarray(np.asarray(L0_attn_Wv).T)
    wo = np.ascontiguousarray(np.asarray(L0_attn_Wo).T)
    wup = np.asarray(L0_ffn_w_up)
    wgate = np.asarray(L0_ffn_w_gate)
    wdown = np.asarray(L0_ffn_w_down)
    etabs = [_make_etab(dt, half) for half in range(2)]
    in_maps = []
    for c in range(NCORE):
        b, half = c // 2, c % 2
        xs = np.zeros((TKV, D), dtype=dt)
        xs[0:TOWN] = x[b, half * TOWN:(half + 1) * TOWN]
        if half == 0:
            xs[TOWN:TKV] = x[b, TOWN:TOWN + 128]
        in_maps.append(dict(
            x_sh=np.ascontiguousarray(xs),
            wqT=wq, wkT=wk, wvT=wv, woT=wo,
            up_sl=np.ascontiguousarray(wup[:, c * 128:(c + 1) * 128]),
            gate_sl=np.ascontiguousarray(wgate[:, c * 128:(c + 1) * 128]),
            down_sl=np.ascontiguousarray(wdown[:, c * 384:(c + 1) * 384]),
            etab=etabs[half],
            ident_d=np.eye(128, dtype=np.float32),
        ))
    return in_maps


def kernel(**inputs):
    in_maps = make_inputs(**inputs)
    nc = build_program(debug=False)
    res = run_bass_kernel_spmd(nc, in_maps, core_ids=list(range(NCORE)))
    out = np.empty((4, 4096, D), dtype=np.float32)
    for c in range(NCORE):
        b, half = c // 2, c % 2
        out[b, half * TOWN:(half + 1) * TOWN] = res.results[c]["out_sh"]
    return out
